# version 21
# baseline (speedup 1.0000x reference)
import numpy as np
import ml_dtypes
from contextlib import ExitStack

import os
import concourse.bass as bass
import concourse.mybir as mybir
from concourse.bass_utils import run_bass_kernel_spmd

F32 = mybir.dt.float32
BF16 = mybir.dt.bfloat16
AF = mybir.ActivationFunctionType
ALU = mybir.AluOpType
AX = mybir.AxisListType.X

MAGIC = float(np.float32(1.5 * 2 ** 23))
STAGE = int(os.environ.get("KSTAGE", "5"))
NB5 = int(os.environ.get("KNB5", "8"))
NHC = int(os.environ.get("KNHC", "4"))
HCPART = int(os.environ.get("KHCPART", "4"))
T = 4096
D = 1024
OWN = 2048
INNER = 4096


def _cp(e, dst, src):
    if hasattr(e, "tensor_copy"):
        return e.tensor_copy(dst, src)
    return e.copy(dst, src)


class Ctr:
    __slots__ = ("h", "v")

    def __init__(self, h):
        self.h = h
        self.v = 0


class Prog:

    ENGS = ("tensor", "vector", "scalar", "gpsimd", "sync")

    def __init__(self):
        self.q = {e: [] for e in self.ENGS}

    def set_auto(self, mapping):
        self.auto = mapping

    def emit(self, eng, fn, waits=(), incs=(), dma=False):
        w = [(c.h, v) for (c, v) in waits]
        incs = list(incs)
        ac = getattr(self, "auto", {}).get(eng)
        if ac is not None and fn is not None and not dma:
            if ac.v > 0:
                w.append((ac.h, ac.v))
            if not any(c is ac for c, _ in incs):
                incs.append((ac, 1))
        ii = []
        mark = None
        for (c, n) in incs:
            c.v += n
            ii.append((c.h, n))
            mark = (c, c.v)
        self.q[eng].append((fn, w, ii))
        return mark

    def replay(self, blk):
        q = self.q
        self.q = {e: [] for e in self.ENGS}

        def runner(ops):
            def run(e):
                for fn, waits, incs in ops:
                    for h, v in waits:
                        e.wait_ge(h, v)
                    if fn is None:
                        assert not incs, "fn=None cannot carry incs"
                        continue
                    inst = fn(e)
                    for h, n in incs:
                        inst.then_inc(h, n)
            return run

        for eng in self.ENGS:
            getattr(blk, eng)(runner(q[eng]))


def build_nc():
    nc = bass.Bass("TRN2", target_bir_lowering=False)

    dx = nc.dram_tensor("x", [T, D], F32, kind="ExternalInput").ap()
    dxo = nc.dram_tensor("xo", [OWN, D], F32, kind="ExternalInput").ap()
    dwq = nc.dram_tensor("wq", [D, D], F32, kind="ExternalInput").ap()
    dwkv = nc.dram_tensor("wkv", [D, 128], F32, kind="ExternalInput").ap()
    dwk = nc.dram_tensor("wk", [128, 512], F32, kind="ExternalInput").ap()
    dwv = nc.dram_tensor("wv", [128, 512], F32, kind="ExternalInput").ap()
    dwo = nc.dram_tensor("woef", [512, D], F32, kind="ExternalInput").ap()
    dwg = nc.dram_tensor("wgT", [D, INNER], F32, kind="ExternalInput").ap()
    dwu = nc.dram_tensor("wuT", [D, INNER], F32, kind="ExternalInput").ap()
    dwd = nc.dram_tensor("wdT", [INNER, D], F32, kind="ExternalInput").ap()
    dqg = nc.dram_tensor("qg4r", [128, 1], F32, kind="ExternalInput").ap()
    dkg = nc.dram_tensor("kg4", [128, 1], F32, kind="ExternalInput").ap()
    dhf = nc.dram_tensor("hflag", [128, 1], F32, kind="ExternalInput").ap()
    dy = nc.dram_tensor("y", [OWN, D], F32, kind="ExternalOutput").ap()

    sg = nc.dram_tensor("scr_g", [D, INNER], BF16, kind="Internal").ap()
    su = nc.dram_tensor("scr_u", [D, INNER], BF16, kind="Internal").ap()
    sd = nc.dram_tensor("scr_d", [INNER, D], BF16, kind="Internal").ap()
    sx1 = nc.dram_tensor("scr_x1", [OWN, D], F32, kind="Internal").ap()

    P = Prog()
    t, v, s, g, y = "tensor", "vector", "scalar", "gpsimd", "sync"

    es = ExitStack()
    with es:
        sb = lambda name, shape, dt: es.enter_context(nc.sbuf_tensor(name, shape, dt))

        ident = sb("ident", [128, 128], BF16)
        maskD = sb("maskD", [128, 256], BF16)
        mtmp = sb("mtmp", [128, 256], BF16)
        ind4T = sb("ind4T", [128, 4], BF16)
        ind4 = sb("ind4", [4, 128], BF16)
        dencol = sb("dencol", [128, 32], BF16)
        qsel01 = sb("qsel01", [128, 4], BF16)
        qsel23 = sb("qsel23", [128, 4], BF16)
        onesf = sb("onesf", [128, 128], F32)
        bcsel = sb("bcsel", [128, 128], BF16)
        qg8 = sb("qg8", [128, 1], F32)
        kg4 = sb("kg4_sb", [128, 1], F32)
        hflag = sb("hflag_sb", [128, 1], F32)
        c127 = sb("c127", [128, 1], F32)
        swb = sb("swb", [128, 4], F32)
        sclw = sb("sclw", [128, 1], F32)
        absacc = sb("absacc", [128, 1], F32)
        negMb = sb("negMb", [128, 1], F32)
        sm = sb("sm", [128, 16], F32)
        sm4 = sb("sm4", [4, 2048], F32)

        cT = Ctr(es.enter_context(nc.semaphore("cT")))
        cA = Ctr(es.enter_context(nc.semaphore("cA")))
        cV = Ctr(es.enter_context(nc.semaphore("cV")))
        cG = Ctr(es.enter_context(nc.semaphore("cG")))
        mksem = lambda n: Ctr(es.enter_context(nc.semaphore(n)))
        qSm = mksem("qSm"); qW = mksem("qW"); qWo = mksem("qWo")
        qF = [mksem("qF0"), mksem("qF1")]
        qP = [mksem("qP0"), mksem("qP1")]
        qX2 = [mksem("qXa"), mksem("qXb")]; qD = mksem("qD"); qM = mksem("qM")
        qXO = mksem("qXO"); qX1 = mksem("qX1"); qXL = mksem("qXL")
        qGU = [mksem("qGU0"), mksem("qGU1")]
        qGUb = [mksem("qGUb0"), mksem("qGUb1")]
        qGUc = [mksem("qGUc0"), mksem("qGUc1")]
        qGUd = [mksem("qGUd0"), mksem("qGUd1")]
        qWD = [mksem("qWD0"), mksem("qWD1")]
        qY = mksem("qY")
        P.set_auto({"vector": cV, "scalar": cA, "gpsimd": cG})

        at = ExitStack()
        with at:
            ab = lambda name, shape, dt: at.enter_context(nc.sbuf_tensor(name, shape, dt))
            kvT = ab("kvT", [128, T], BF16)
            khat = ab("khat", [128, 4 * T], BF16)
            vtok = ab("vtok", [128, 32 * 512], BF16)
            qn = ab("qn", [128, 4 * OWN], BF16)
            kmax2 = ab("kmax2", [4, 4], F32)
            qmax2 = ab("qmax2", [4, 4], F32)

            s1 = ExitStack()
            with s1:
                b1 = lambda name, shape, dt: s1.enter_context(nc.sbuf_tensor(name, shape, dt))
                wq_sb = b1("wq_sb", [128, 8 * D], BF16)
                wkv_sb = b1("wkv_sb", [128, 8 * 128], BF16)
                wk_sb = b1("wk_sb", [128, 512], BF16)
                wv_sb = b1("wv_sb", [128, 512], BF16)
                xst2 = [b1("xst0", [128, 4 * D], BF16),
                        b1("xst1", [128, 4 * D], BF16)]
                xTc2 = [b1("xTc0", [128, 8 * 512], BF16),
                        b1("xTc1", [128, 8 * 512], BF16)]
                krbuf = b1("krbuf", [128, 512], BF16)
                kr2buf = b1("kr2buf", [128, 512], BF16)
                krall = b1("krall", [128, 4 * 512], BF16)
                vtbuf = b1("vtbuf", [128, 512], BF16)
                qfull = b1("qfull", [128, 512], BF16)
                q2buf = b1("q2buf", [128, 2 * 512], BF16)
                rkc = b1("rkc", [4, 512], F32)
                rkh = b1("rkh", [4, 512], BF16)
                rkl = b1("rkl", [4, 512], BF16)
                wfbuf = [b1("wfbuf0", [128, INNER], F32),
                         b1("wfbuf1", [128, INNER], F32)]
                wqbuf_s = b1("wqbuf_s", [128, INNER], BF16)
                wqbuf = [wqbuf_s, wqbuf_s]

                ps1 = ExitStack()
                with ps1:
                    pb = lambda name, shape, dt: ps1.enter_context(nc.psum_tensor(name, shape, dt))
                    ps_a = pb("ps_a", [128, 512], F32)
                    ps_b = pb("ps_b", [128, 512], F32)
                    ps_ss = pb("ps_ss", [4, 512], F32)
                    ps_bc = pb("ps_bc", [128, 512], F32)
                    ps_tr = [pb("ps_tr0", [128, 1024], BF16),
                             pb("ps_tr1", [128, 1024], BF16)]
                    ps_vt = pb("ps_vt", [128, 512], BF16)
                    ps_sw = pb("ps_sw", [128, 1], F32)

                    P.emit(y, lambda e: e.dma_start(out=qg8[:], in_=dqg), incs=[(qSm, 16)])
                    P.emit(y, lambda e: e.dma_start(out=kg4[:], in_=dkg), incs=[(qSm, 16)])
                    mk_small = P.emit(y, lambda e: e.dma_start(out=hflag[:], in_=dhf),
                                      incs=[(qSm, 16)])
                    _gprev = None
                    def _gchain(fn):
                        global _unused
                        nonlocal _gprev
                        _gprev = P.emit(g, fn, waits=([_gprev] if _gprev else []), incs=[(cG, 1)])
                    _gchain(lambda e: e.memset(ident[:], 1.0))
                    _gchain(lambda e: e.affine_select(ident[:], ident[:], [[1, 128]], ALU.is_equal, 0.0, base=0, channel_multiplier=-1))
                    _gchain(lambda e: e.memset(mtmp[:], 1.0))
                    _gchain(lambda e: e.affine_select(mtmp[:, 0:128], mtmp[:, 0:128], [[1, 128]], ALU.is_ge, 0.0, base=0, channel_multiplier=-1))
                    _gchain(lambda e: e.memset(mtmp[:, 128:256], 0.0))
                    _gchain(lambda e: e.memset(maskD[:], 1.0))
                    _gchain(lambda e: e.affine_select(maskD[:, 128:256], maskD[:, 128:256], [[1, 128]], ALU.is_ge, 0.0, base=0, channel_multiplier=-1))
                    _gchain(lambda e: e.memset(ind4T[:], 1.0))
                    _gchain(lambda e: e.affine_select(ind4T[:], ind4T[:], [[-32, 4]], ALU.is_ge, 0.0, base=0, channel_multiplier=1))
                    _gchain(lambda e: e.affine_select(ind4T[:], ind4T[:], [[32, 4]], ALU.is_ge, 0.0, base=31, channel_multiplier=-1))
                    _gchain(lambda e: e.memset(ind4[:], 1.0))
                    _gchain(lambda e: e.affine_select(ind4[:], ind4[:], [[1, 128]], ALU.is_ge, 0.0, base=0, channel_multiplier=-32))
                    _gchain(lambda e: e.affine_select(ind4[:], ind4[:], [[-1, 128]], ALU.is_ge, 0.0, base=31, channel_multiplier=32))
                    _gchain(lambda e: e.memset(dencol[:], 0.0))
                    _gchain(lambda e: e.memset(dencol[:, 0:1], 1.0))
                    _gchain(lambda e: e.memset(qsel01[:], 0.0))
                    _gchain(lambda e: e.memset(qsel01[0:64, 0:1], 1.0))
                    _gchain(lambda e: e.memset(qsel01[64:128, 1:2], 1.0))
                    _gchain(lambda e: e.memset(qsel23[:], 0.0))
                    _gchain(lambda e: e.memset(qsel23[0:64, 2:3], 1.0))
                    _gchain(lambda e: e.memset(qsel23[64:128, 3:4], 1.0))
                    _gchain(lambda e: e.memset(onesf[:], 1.0))
                    _gchain(lambda e: e.memset(c127[:], 127.0))
                    _gchain(lambda e: e.memset(kmax2[:], 0.0))
                    _gchain(lambda e: e.memset(qmax2[:], 0.0))
                    _gchain(lambda e: e.memset(bcsel[:], 1.0))
                    for _j in range(4):
                        _gchain(lambda e, _j=_j: e.affine_select(
                            bcsel[:, 32 * _j:32 * _j + 32], bcsel[:, 32 * _j:32 * _j + 32],
                            [[0, 32]], ALU.is_equal, 0.0, base=-32 * _j,
                            channel_multiplier=1))
                    mk_const = _gprev
                    P.emit(v, lambda e: e.tensor_sub(maskD[:], maskD[:], mtmp[:]),
                           waits=[(qSm, mk_small[1]), (cG, mk_const[1])])
                    P.emit(v, lambda e: e.scalar_tensor_tensor(
                        maskD[:], maskD[:], hflag[:, 0:1], mtmp[:], ALU.mult, ALU.add))
                    P.emit(v, lambda e: e.tensor_scalar_mul(qg8[:], qg8[:], 0.125),
                           incs=[(cV, 1)])
                    P.emit(g, lambda e: e.dma_start(
                        out=wq_sb.rearrange("p (c n) -> p c n", c=8),
                        in_=dwq.rearrange("(c p) n -> p c n", p=128)), incs=[(qW, 16)], dma=True)
                    P.emit(g, lambda e: e.dma_start(
                        out=wkv_sb.rearrange("p (c n) -> p c n", c=8),
                        in_=dwkv.rearrange("(c p) n -> p c n", p=128)), incs=[(qW, 16)], dma=True)
                    P.emit(g, lambda e: e.dma_start(out=wk_sb[:], in_=dwk), incs=[(qW, 16)], dma=True)
                    mk_wload = P.emit(g, lambda e: e.dma_start(out=wv_sb[:], in_=dwv),
                                      incs=[(qW, 16)], dma=True)

                    def f0_emitter():
                        dmk = [None, None]
                        for wi, (src, dst, nt, ncols) in enumerate(
                                [(dwg, sg, 8, INNER), (dwu, su, 8, INNER), (dwd, sd, 32, D)]
                                if STAGE >= 1 else []):
                            mk_z = P.emit(g, lambda e: e.memset(absacc[:], 0.0),
                                          waits=[(cV, cV.v)], incs=[(cG, 1)])
                            vmk = [None, None]
                            for i in range(nt):
                                bi = i % 2
                                w = [(cG, mk_z[1])] if i < 2 else []
                                if dmk[bi] is not None:
                                    w.append((qP[bi], dmk[bi][1]))
                                if vmk[bi] is not None:
                                    w.append((cV, vmk[bi][1]))
                                mkd = P.emit(y if bi == 0 else "scalar",
                                             lambda e, i=i, bi=bi, src=src, ncols=ncols:
                                             e.dma_start(out=wfbuf[bi][:, 0:ncols],
                                                         in_=src[i * 128:(i + 1) * 128, :]),
                                             waits=w, incs=[(qF[bi], 16)], dma=True)
                                wsm = [(qF[bi], mkd[1])]
                                wsm += [(qP[k], dmk[k][1]) for k in range(2) if dmk[k]]
                                if vmk[0] is not None:
                                    wsm.append(vmk[0])
                                if vmk[1] is not None:
                                    wsm.append(vmk[1])
                                mka = P.emit(s, lambda e, bi=bi, ncols=ncols: e.activation(
                                    wqbuf[bi][:, 0:ncols], wfbuf[bi][:, 0:ncols], AF.Abs,
                                    accum_out=sm[:, 0:1]), waits=wsm,
                                    incs=[(cA, 1)])
                                vmk[bi] = P.emit(v, lambda e: e.tensor_add(
                                    absacc[:], absacc[:], sm[:, 0:1]),
                                    waits=[(cA, mka[1])], incs=[(cV, 1)])
                                yield
                            mkmm = P.emit(t, lambda e: e.matmul(
                                ps_sw[:], onesf[:], absacc[:], start=True, stop=True),
                                waits=[(cV, cV.v)], incs=[(cT, 1)])
                            P.emit(v, lambda e, wi=wi: e.tensor_scalar(
                                swb[:, wi:wi + 1], ps_sw[:], 1.0 / float(INNER * D), 1e-5,
                                ALU.mult, ALU.max), waits=[(cT, mkmm[1])])
                            mksc = P.emit(v, lambda e, wi=wi: e.reciprocal(
                                sclw[:], swb[:, wi:wi + 1]), waits=[(cA, cA.v)],
                                incs=[(cV, 1)])
                            yield
                            for i in range(nt):
                                bi = i % 2
                                w = [(cV, mksc[1])]
                                if dmk[bi] is not None:
                                    w.append((qP[bi], dmk[bi][1]))
                                mkd = P.emit(y if bi == 0 else "scalar",
                                             lambda e, i=i, bi=bi, src=src, ncols=ncols:
                                             e.dma_start(out=wfbuf[bi][:, 0:ncols],
                                                         in_=src[i * 128:(i + 1) * 128, :]),
                                             waits=w, incs=[(qF[bi], 16)], dma=True)
                                P.emit(v, lambda e, bi=bi, ncols=ncols: e.tensor_scalar(
                                    wfbuf[bi][:, 0:ncols], wfbuf[bi][:, 0:ncols],
                                    sclw[:, 0:1], MAGIC, ALU.mult, ALU.add),
                                    waits=[(qF[bi], mkd[1])])
                                P.emit(v, lambda e, bi=bi, ncols=ncols: e.tensor_scalar(
                                    wfbuf[bi][:, 0:ncols], wfbuf[bi][:, 0:ncols],
                                    MAGIC, 1.0, ALU.subtract, ALU.min))
                                wv2 = [(qP[k], dmk[k][1]) for k in range(2) if dmk[k]]
                                vmq = P.emit(v, lambda e, bi=bi, ncols=ncols: e.tensor_scalar_max(
                                    wqbuf[bi][:, 0:ncols], wfbuf[bi][:, 0:ncols], -1.0),
                                    waits=wv2, incs=[(cV, 1)])
                                dmk[bi] = P.emit(y, lambda e, i=i, bi=bi, dst=dst, ncols=ncols:
                                                 e.dma_start(out=dst[i * 128:(i + 1) * 128, :],
                                                             in_=wqbuf[bi][:, 0:ncols]),
                                                 waits=[(cV, vmq[1])], incs=[(qP[bi], 16)])
                                yield
                        mk_f0_v = P.emit(v, lambda e: e.engine_nop(), incs=[(cV, 1)])
                        yield
                    f0_gen = f0_emitter()
                    f0_done = [False]

                    def f0_step(n=1):
                        for _ in range(n):
                            if not f0_done[0]:
                                try:
                                    next(f0_gen)
                                except StopIteration:
                                    f0_done[0] = True

                    tr_evac = [None, None]

                    def tr_batch(srcs, dst_fn, extra_wait, bank):
                        for i4 in range(4):
                            w = []
                            if i4 == 0:
                                w += extra_wait
                                if tr_evac[bank] is not None:
                                    w.append(tr_evac[bank])
                            P.emit(t, lambda e, i4=i4, bank=bank, src=srcs[i4]:
                                   e.transpose(ps_tr[bank][:, i4 * 128:(i4 + 1) * 128],
                                               src, ident[:]),
                                   waits=w, incs=([(cT, 1)] if i4 == 3 else []))
                        eng, ctr = (v, cV) if bank == 0 else (s, cA)
                        mk = P.emit(eng, lambda e, bank=bank: dst_fn(e, ps_tr[bank]),
                                    waits=[(cT, cT.v)], incs=[(ctr, 1)])
                        tr_evac[bank] = mk
                        return mk

                    mk_trdone = [None, None]
                    for c5 in range((8 if STAGE > 2 else NB5) if STAGE >= 2 else 0):
                        s0 = c5 * 512
                        par5 = c5 % 2
                        xst = xst2[par5]
                        xTc = xTc2[par5]
                        w = [(qW, mk_wload[1])] if c5 == 0 else []
                        if c5 < 2:
                            w += [(qP[0], qP[0].v), (qP[1], qP[1].v)]
                        if mk_trdone[par5] is not None:
                            w.append(mk_trdone[par5])
                        mk_x = P.emit(g, lambda e, s0=s0, xst=xst: e.dma_start(
                            out=xst.rearrange("p (n f) -> p n f", n=4),
                            in_=dx[s0:s0 + 512, :].rearrange("(n p) f -> p n f", p=128)),
                            waits=w, incs=[(qX2[par5], 16)], dma=True)
                        for c in range(8):
                            srcs = [xst[:, i4 * D + c * 128:i4 * D + c * 128 + 128]
                                    for i4 in range(4)]
                            ew = [(qX2[par5], mk_x[1])] if c == 0 else []
                            if c == 0 and c5 == 0:
                                ew.append((cG, cG.v))
                            tr_batch(srcs,
                                     (lambda c, xTc=xTc: lambda e, psl: _cp(e,
                                         xTc[:, c * 512:(c + 1) * 512],
                                         psl[:, 0:512]))(c),
                                     ew, c % 2)
                        mk_trdone[par5] = (cT, cT.v)
                        for c in range(8):
                            w = [(cV, cV.v), (cA, cA.v)] if c == 0 else []
                            P.emit(t, lambda e, c=c, xTc=xTc: e.matmul(
                                ps_a[:], wkv_sb[:, c * 128:(c + 1) * 128],
                                xTc[:, c * 512:(c + 1) * 512],
                                start=(c == 0), stop=(c == 7)), waits=w,
                                incs=([(cT, 1)] if c == 7 else []))
                        mk_kv = P.emit(v, lambda e, s0=s0: e.tensor_copy(
                            kvT[:, s0:s0 + 512], ps_a[:]), waits=[(cT, cT.v)],
                            incs=[(cV, 1)])
                        f0_step(1)
                        mk_vread = [mk_kv, None]
                        mk2_prev = None
                        mk_vtok_prev = None
                        for hc in range(NHC):
                            co = hc * T + s0
                            pk = ps_b if hc % 2 == 0 else ps_a
                            pvp = ps_a if hc % 2 == 0 else ps_b
                            kb = 1 - (hc % 2)
                            vb = hc % 2
                            w1 = [(cV, mk_kv[1])]
                            if mk_vread[kb] is not None:
                                w1.append(mk_vread[kb])
                            mk1 = P.emit(t, lambda e, hc=hc, s0=s0, pk=pk: e.matmul(
                                pk[:], wk_sb[:, hc * 128:(hc + 1) * 128],
                                kvT[:, s0:s0 + 512], start=True, stop=True),
                                waits=w1, incs=[(cT, 1)])
                            wkrc = [(cT, mk1[1])]
                            if mk2_prev is not None:
                                wkrc.append(mk2_prev)
                            mk_krc = P.emit(s, lambda e, pk=pk: _cp(
                                e, krbuf[:], pk[:]), waits=wkrc, incs=[(cA, 1)])
                            mk_vread[kb] = mk_krc
                            P.emit(v, lambda e: e.tensor_mul(
                                kr2buf[:], krbuf[:], krbuf[:]),
                                waits=[(cA, mk_krc[1])])
                            mk2 = P.emit(v, lambda e, hc=hc: e.tensor_copy(
                                krall[:, hc * 512:hc * 512 + 512], krbuf[:]),
                                incs=[(cV, 1)])
                            mk2_prev = (cV, mk2[1])
                            if HCPART < 1:
                                continue
                            mk3 = P.emit(t, lambda e: e.matmul(
                                ps_ss[:], ind4T[:], kr2buf[:], start=True, stop=True),
                                waits=[(cV, mk2[1])], incs=[(cT, 1)])
                            P.emit(v, lambda e: e.tensor_scalar(
                                sm4[:, 0:512], ps_ss[:], 1.0 / 32.0, 1e-6,
                                ALU.mult, ALU.add), waits=[(cT, mk3[1])], incs=[(cV, 1)])
                            P.emit(s, lambda e: e.activation(
                                sm4[:, 512:1024], sm4[:, 0:512], AF.Ln),
                                waits=[(cV, cV.v)])
                            mke = P.emit(s, lambda e: e.activation(
                                sm4[:, 1024:1536], sm4[:, 512:1024], AF.Exp, scale=-0.5),
                                incs=[(cA, 1)])
                            P.emit(v, lambda e: e.tensor_mul(
                                sm4[:, 512:1024], sm4[:, 1024:1536], sm4[:, 1024:1536]),
                                waits=[(cA, mke[1])])
                            P.emit(v, lambda e: e.tensor_mul(
                                sm4[:, 512:1024], sm4[:, 512:1024], sm4[:, 0:512]))
                            P.emit(v, lambda e: e.tensor_scalar(
                                sm4[:, 512:1024], sm4[:, 512:1024], -0.5, 1.5,
                                ALU.mult, ALU.add))
                            P.emit(v, lambda e: e.tensor_mul(
                                rkc[:], sm4[:, 1024:1536], sm4[:, 512:1024]))
                            P.emit(v, lambda e: e.tensor_copy(rkh[:], rkc[:]))
                            mk4 = P.emit(v, lambda e: e.tensor_sub(
                                rkl[:], rkc[:], rkh[:]), incs=[(cV, 1)])
                            if HCPART < 2:
                                continue
                            P.emit(t, lambda e: e.matmul(
                                ps_bc[:], ind4[:], rkh[:],
                                start=True, stop=False),
                                waits=[(cV, mk4[1])])
                            mk5 = P.emit(t, lambda e: e.matmul(
                                ps_bc[:], ind4[:], rkl[:],
                                start=False, stop=True), incs=[(cT, 1)])
                            P.emit(v, lambda e, co=co, hc=hc: e.scalar_tensor_tensor(
                                khat[:, co:co + 512], krall[:, hc * 512:hc * 512 + 512],
                                kg4[:, 0:1], ps_bc[:], ALU.mult, ALU.mult),
                                waits=[(cT, mk5[1])], incs=[(cV, 1)])
                            if HCPART < 3:
                                continue
                            mk7 = P.emit(v, lambda e, co=co: e.tensor_mul(
                                kr2buf[:], khat[:, co:co + 512], khat[:, co:co + 512]),
                                incs=[(cV, 1)])
                            mk8 = P.emit(t, lambda e: e.matmul(
                                ps_ss[:], ind4T[:], kr2buf[:], start=True, stop=True),
                                waits=[(cV, mk7[1])], incs=[(cT, 1)])
                            P.emit(v, lambda e: e.tensor_reduce(
                                sm4[:, 0:1], ps_ss[:], AX, ALU.max), waits=[(cT, mk8[1])])
                            P.emit(v, lambda e, hc=hc: e.tensor_max(
                                kmax2[:, hc:hc + 1], kmax2[:, hc:hc + 1], sm4[:, 0:1]),
                                incs=[(cV, 1)])
                            if HCPART < 4:
                                continue
                            w9 = []
                            if mk_vread[vb] is not None:
                                w9.append(mk_vread[vb])
                            mk9 = P.emit(t, lambda e, hc=hc, s0=s0, pvp=pvp: e.matmul(
                                pvp[:], wv_sb[:, hc * 128:(hc + 1) * 128],
                                kvT[:, s0:s0 + 512], start=True, stop=True),
                                waits=w9, incs=[(cT, 1)])
                            mk10 = P.emit(s, lambda e, pvp=pvp: _cp(
                                e, vtbuf[:], pvp[:]), waits=[(cT, mk9[1])],
                                incs=[(cA, 1)])
                            mk_vread[vb] = mk10
                            for ss in range(4):
                                w = [mk10] if ss == 0 else []
                                if ss == 0 and mk_vtok_prev is not None:
                                    w.append(mk_vtok_prev)
                                P.emit(t, lambda e, ss=ss: e.transpose(
                                    ps_vt[:, ss * 128:(ss + 1) * 128],
                                    vtbuf[:, ss * 128:(ss + 1) * 128], ident[:]),
                                    waits=w, incs=([(cT, 1)] if ss == 3 else []))
                            mk_vtok = P.emit(s, lambda e, c5=c5, hc=hc: _cp(
                                e,
                                vtok.rearrange("p (a n) -> p a n", n=512)
                                [:, 4 * c5:4 * c5 + 4, hc * 128:hc * 128 + 128],
                                ps_vt.rearrange("p (a n) -> p a n", n=128)),
                                waits=[(cT, cT.v)], incs=[(cA, 1)])
                            mk_vtok_prev = (cA, mk_vtok[1])
                            f0_step(2)

                    f0_step(200)
                    mk_qevac = [None, None]
                    for u in range(4 if STAGE >= 3 else 0):
                        par5 = u % 2
                        xst = xst2[par5]
                        xTc = xTc2[par5]
                        w = []
                        if mk_trdone[par5] is not None:
                            w.append(mk_trdone[par5])
                        mk_x = P.emit(g, lambda e, u=u, xst=xst: e.dma_start(
                            out=xst.rearrange("p (n f) -> p n f", n=4),
                            in_=dxo[u * 512:(u + 1) * 512, :]
                            .rearrange("(n p) f -> p n f", p=128)),
                            waits=w, incs=[(qX2[par5], 16)], dma=True)
                        for c in range(8):
                            srcs = [xst[:, i4 * D + c * 128:i4 * D + c * 128 + 128]
                                    for i4 in range(4)]
                            ew = [(qX2[par5], mk_x[1])] if c == 0 else []
                            tr_batch(srcs,
                                     (lambda c, xTc=xTc: lambda e, psl: _cp(e,
                                         xTc[:, c * 512:(c + 1) * 512],
                                         psl[:, 0:512]))(c),
                                     ew, c % 2)
                        mk_trdone[par5] = (cT, cT.v)
                        for gi in range(4):
                            mkq2 = None
                            for half in range(2):
                                cc = 2 * gi + half
                                pq = ps_a if cc % 2 == 0 else ps_b
                                for c in range(8):
                                    w = []
                                    if c == 0:
                                        if gi == 0 and half == 0:
                                            w = [(cV, cV.v), (cA, cA.v)]
                                        if mk_qevac[cc % 2] is not None:
                                            w.extend(mk_qevac[cc % 2])
                                    P.emit(t, lambda e, c=c, cc=cc, xTc=xTc, pq=pq:
                                           e.matmul(
                                               pq[:],
                                               wq_sb[:, c * D + cc * 128:
                                                     c * D + cc * 128 + 128],
                                               xTc[:, c * 512:(c + 1) * 512],
                                               start=(c == 0), stop=(c == 7)), waits=w,
                                           incs=([(cT, 1)] if c == 7 else []))
                                mkq = (cT, cT.v)
                                j0 = 2 * half
                                P.emit(v, lambda e, gi=gi, u=u, j0=j0, pq=pq:
                                       e.tensor_copy(
                                           qn[32 * j0:32 * j0 + 32,
                                              gi * OWN + u * 512:
                                              gi * OWN + u * 512 + 512],
                                           pq[0:32, :]), waits=[mkq], incs=[(cV, 1)])
                                P.emit(v, lambda e, gi=gi, u=u, j0=j0, pq=pq:
                                       e.tensor_copy(
                                           qn[32 * (j0 + 1):32 * (j0 + 1) + 32,
                                              gi * OWN + u * 512:
                                              gi * OWN + u * 512 + 512],
                                           pq[64:96, :]), incs=[(cV, 1)])
                                P.emit(v, lambda e, pq=pq: e.tensor_copy(
                                    qfull[:], pq[:]), waits=[mkq])
                                mkq2 = P.emit(v, lambda e, half=half: e.tensor_mul(
                                    q2buf[:, half * 512:half * 512 + 512],
                                    qfull[:], qfull[:]), incs=[(cV, 1)])
                                mk_qevac[cc % 2] = [(cV, cV.v), (cA, cA.v)]
                            P.emit(t, lambda e: e.matmul(
                                ps_ss[:], qsel01[:], q2buf[:, 0:512],
                                start=True, stop=False), waits=[(cV, mkq2[1])])
                            mkss = P.emit(t, lambda e: e.matmul(
                                ps_ss[:], qsel23[:], q2buf[:, 512:1024],
                                start=False, stop=True), incs=[(cT, 1)])
                            P.emit(v, lambda e: e.tensor_scalar(
                                sm4[:, 0:512], ps_ss[:], 1.0 / 64.0, 1e-6,
                                ALU.mult, ALU.add), waits=[(cT, mkss[1])],
                                incs=[(cV, 1)])
                            P.emit(s, lambda e: e.activation(
                                sm4[:, 512:1024], sm4[:, 0:512], AF.Ln),
                                waits=[(cV, cV.v)])
                            mke = P.emit(s, lambda e: e.activation(
                                sm4[:, 1024:1536], sm4[:, 512:1024], AF.Exp,
                                scale=-0.5), incs=[(cA, 1)])
                            P.emit(v, lambda e: e.tensor_mul(
                                sm4[:, 512:1024], sm4[:, 1024:1536],
                                sm4[:, 1024:1536]), waits=[(cA, mke[1])])
                            P.emit(v, lambda e: e.tensor_mul(
                                sm4[:, 512:1024], sm4[:, 512:1024], sm4[:, 0:512]))
                            P.emit(v, lambda e: e.tensor_scalar(
                                sm4[:, 512:1024], sm4[:, 512:1024], -0.5, 1.5,
                                ALU.mult, ALU.add))
                            P.emit(v, lambda e: e.tensor_mul(
                                rkc[:], sm4[:, 1024:1536], sm4[:, 512:1024]))
                            P.emit(v, lambda e: e.tensor_copy(rkh[:], rkc[:]))
                            mk4 = P.emit(v, lambda e: e.tensor_sub(
                                rkl[:], rkc[:], rkh[:]), incs=[(cV, 1)])
                            co = gi * OWN + u * 512
                            P.emit(t, lambda e: e.matmul(
                                ps_bc[:], ind4[:], rkh[:], start=True, stop=False),
                                waits=[(cV, mk4[1])])
                            mk5 = P.emit(t, lambda e: e.matmul(
                                ps_bc[:], ind4[:], rkl[:], start=False, stop=True),
                                incs=[(cT, 1)])
                            P.emit(v, lambda e, co=co: e.scalar_tensor_tensor(
                                qn[:, co:co + 512], qn[:, co:co + 512], qg8[:, 0:1],
                                ps_bc[:], ALU.mult, ALU.mult),
                                waits=[(cT, mk5[1])], incs=[(cV, 1)])
                            mk7 = P.emit(v, lambda e, co=co: e.tensor_mul(
                                qfull[:], qn[:, co:co + 512], qn[:, co:co + 512]),
                                incs=[(cV, 1)])
                            mk8 = P.emit(t, lambda e: e.matmul(
                                ps_ss[:], ind4T[:], qfull[:], start=True, stop=True),
                                waits=[(cV, mk7[1])], incs=[(cT, 1)])
                            P.emit(v, lambda e: e.tensor_reduce(
                                sm4[:, 0:1], ps_ss[:], AX, ALU.max),
                                waits=[(cT, mk8[1])])
                            P.emit(v, lambda e, gi=gi: e.tensor_max(
                                qmax2[:, gi:gi + 1], qmax2[:, gi:gi + 1], sm4[:, 0:1]),
                                incs=[(cV, 1)])
                    if STAGE < 3:
                        P.emit(y, None, waits=[(qX2[0], qX2[0].v), (qW, qW.v)])
                    else:
                        mkp = P.emit(v, lambda e: e.tensor_mul(sm4[:, 0:4], qmax2[:], kmax2[:]),
                                     incs=[(cV, 1)])
                        mkg = P.emit(y, lambda e: e.dma_start(
                            out=sm[0:1, 0:16].rearrange("p (a b) -> p a b", a=4),
                            in_=sm4[:, 0:4].rearrange("p (n o) -> p n o", o=1),
                            allow_slow_non_contiguous=True),
                            waits=[(cV, mkp[1])], incs=[(qM, 16)])
                        P.emit(v, lambda e: e.tensor_reduce(
                            sm4[0:1, 0:1], sm[0:1, 0:16], AX, ALU.max),
                            waits=[(qM, mkg[1])], incs=[(cV, 1)])
                        P.emit(s, lambda e: e.activation(sm[0:1, 1:2], sm4[0:1, 0:1], AF.Ln),
                               waits=[(cV, cV.v)])
                        mks = P.emit(s, lambda e: e.activation(
                            sm[0:1, 2:3], sm[0:1, 1:2], AF.Exp, scale=0.5),
                            incs=[(cA, 1)])
                        mkneg = P.emit(v, lambda e: e.tensor_scalar_mul(
                            sm[0:1, 3:4], sm[0:1, 2:3], -1.01),
                            waits=[(cA, mks[1])], incs=[(cV, 1)])
                        mkb = P.emit(t, lambda e: e.matmul(
                            ps_sw[:], onesf[0:1, :], sm[0:1, 3:4],
                            start=True, stop=True), waits=[(cV, mkneg[1])], incs=[(cT, 1)])
                        P.emit(v, lambda e: e.tensor_copy(negMb[:], ps_sw[:]),
                               waits=[(cT, mkb[1])], incs=[(cV, 1)])

                    with nc.Block() as blk:
                        P.replay(blk)

            s2 = ExitStack()
            with s2:
                b2 = lambda name, shape, dt: s2.enter_context(nc.sbuf_tensor(name, shape, dt))
                woef_sb = b2("woef_sb", [128, 4 * D], BF16)
                PT = b2("PT", [128, 2048], BF16)
                PTb = b2("PTb", [128, 2048], BF16)
                outcT = b2("outcT", [128, 2048], BF16)
                den4h2 = b2("den4h2", [128, 512], BF16)
                den4l2 = b2("den4l2", [128, 512], BF16)
                den_sb = b2("den_sb", [128, 512], F32)
                bc_sb = b2("bc_sb", [128, 512], F32)
                xo_t = b2("xo_t", [128, 1024], F32)
                r_t = b2("r_t", [128, 1024], F32)
                x1_t = b2("x1_t", [128, 1024], F32)

                ps2 = ExitStack()
                with ps2:
                    pb = lambda name, shape, dt: ps2.enter_context(nc.psum_tensor(name, shape, dt))
                    ps_sc = pb("ps_sc", [128, 2048], F32)
                    ps_den = pb("ps_den", [128, 512], F32)
                    ps_pv = pb("ps_pv", [128, 512], F32)
                    ps_bc2 = pb("ps_bc2", [128, 512], F32)
                    ps_wo = pb("ps_wo", [128, 512], F32)

                    mk_wo = P.emit(g, lambda e: e.dma_start(
                        out=woef_sb.rearrange("p (c n) -> p c n", c=4),
                        in_=dwo.rearrange("(c p) n -> p c n", p=128)), incs=[(qWo, 16)], dma=True)

                    PTr = [PT.rearrange("p (j n) -> p j n", j=4),
                           PTb.rearrange("p (j n) -> p j n", j=4)]
                    PTt = [PT, PTb]
                    PSr = ps_sc.rearrange("p (j n) -> p j n", j=4)

                    mk_x1st = None
                    mk_exp_last = None
                    bnd = {"fden": None, "fpv": None}
                    chain_pend = None
                    woln_pend = []
                    for tc in range(4 if STAGE >= 4 else 0):
                        mk_exp = None
                        for gi in range(4):
                            smax = 8 * tc + 8
                            pend = None
                            mk_flush = [None, None]
                            mk_expA = None
                            for sc in range(smax):
                                j0 = sc // 2
                                stc = 128 * (j0 - 4 * tc) if sc >= 8 * tc else 0
                                pp = sc % 2
                                for j in range(4):
                                    w = []
                                    if j == 0:
                                        if mk_expA is not None:
                                            w.append(mk_expA)
                                        elif mk_exp_last is not None:
                                            w.append(mk_exp_last)
                                        else:
                                            w.append((cV, cV.v))
                                    if j == 2 and mk_exp is not None:
                                        w.append(mk_exp)
                                    P.emit(t, lambda e, j=j, gi=gi, tc=tc, sc=sc, stc=stc:
                                           e.matmul(
                                               ps_sc[:, j * 512 + stc:j * 512 + 512],
                                               khat[32 * j:32 * j + 32,
                                                    gi * T + sc * 128:
                                                    gi * T + sc * 128 + 128],
                                               qn[32 * j:32 * j + 32,
                                                  gi * OWN + tc * 512 + stc:
                                                  gi * OWN + (tc + 1) * 512],
                                               start=True, stop=True,
                                               tile_position=(32 * j, 0)),
                                           waits=w,
                                           incs=([(cT, 1)] if j in (1, 3) else []))
                                mk_scA = (cT, cT.v - 1)
                                mk_scB = (cT, cT.v)
                                if sc == 1 and chain_pend is not None:
                                    chain_pend()
                                    chain_pend = None
                                if pend is not None:
                                    pend()
                                    pend = None
                                    mk_flush[1 - pp] = (cT, cT.v)
                                w = [mk_scA]
                                if mk_flush[pp] is not None:
                                    w.append(mk_flush[pp])
                                mk_expA = P.emit(s, lambda e, stc=stc, pp=pp: e.activation(
                                    PTr[pp][:, 0:2, stc:512], PSr[:, 0:2, stc:512],
                                    AF.Exp, bias=negMb[:, 0:1]),
                                    waits=w, incs=[(cA, 1)])
                                mk_exp = P.emit(s, lambda e, stc=stc, pp=pp: e.activation(
                                    PTr[pp][:, 2:4, stc:512], PSr[:, 2:4, stc:512],
                                    AF.Exp, bias=negMb[:, 0:1]),
                                    waits=[mk_scB], incs=[(cA, 1)])
                                dep = mk_exp
                                if sc >= 8 * tc:
                                    mo = 128 * (sc % 2)
                                    dep = P.emit(v, lambda e, stc=stc, mo=mo, pp=pp:
                                                 e.tensor_tensor(
                                                     PTr[pp][:, :, stc:stc + 128],
                                                     PTr[pp][:, :, stc:stc + 128],
                                                     maskD[:, mo:mo + 128]
                                                     .rearrange("p (r n) -> p r n", r=1)
                                                     .broadcast_to((128, 4, 128)),
                                                     ALU.mult),
                                                 waits=[mk_exp], incs=[(cV, 1)])

                                def mk_pend(gi=gi, sc=sc, stc=stc, smax=smax, pp=pp,
                                            dep=dep):
                                    def emit_dp():
                                        fden = bnd["fden"]
                                        fpv = bnd["fpv"]
                                        for j in range(4):
                                            w = [dep] if j == 0 else []
                                            if j == 0 and sc == 0 and fden is not None:
                                                w.append(fden)
                                            P.emit(t, lambda e, j=j, sc=sc, stc=stc,
                                                   smax=smax, pp=pp: e.matmul(
                                                       ps_den[32 * j:32 * j + 32,
                                                              stc:512],
                                                       dencol[:],
                                                       PTt[pp][:, j * 512 + stc:
                                                               j * 512 + 512],
                                                       start=(sc == 0),
                                                       stop=(sc == smax - 1),
                                                       skip_group_check=True,
                                                       tile_position=(0, 32 * j)),
                                                   waits=w)
                                        for j in range(4):
                                            w = []
                                            if j == 0 and sc == 0 and fpv is not None:
                                                w.append(fpv)
                                            P.emit(t, lambda e, j=j, gi=gi, sc=sc,
                                                   stc=stc, smax=smax, pp=pp: e.matmul(
                                                       ps_pv[32 * j:32 * j + 32,
                                                             stc:512],
                                                       vtok[:, sc * 512
                                                            + (4 * gi + j) * 32:
                                                            sc * 512
                                                            + (4 * gi + j) * 32 + 32],
                                                       PTt[pp][:, j * 512 + stc:
                                                               j * 512 + 512],
                                                       start=(sc == 0),
                                                       stop=(sc == smax - 1),
                                                       skip_group_check=True,
                                                       tile_position=(0, 32 * j)),
                                                   waits=w,
                                                   incs=([(cT, 1)] if j == 3 else []))
                                    return emit_dp
                                pend = mk_pend()
                                if gi == 0 and 2 <= sc <= 5 and woln_pend:
                                    woln_pend.pop(0)()
                            pend()
                            pend = None
                            mk_exp_last = mk_exp
                            mk_exp = None
                            mk_mmd = (cT, cT.v)

                            def make_chain(gi=gi, mk_mmd=mk_mmd):
                                def emit_chain():
                                    mk_denevac = P.emit(
                                        v, lambda e: e.tensor_scalar_max(
                                            den_sb[:], ps_den[:], 1e-30),
                                        waits=[mk_mmd], incs=[(cV, 1)])
                                    P.emit(v, lambda e: e.reciprocal(
                                        den_sb[:], den_sb[:]))
                                    P.emit(v, lambda e: e.tensor_copy(
                                        den4h2[:], den_sb[:]))
                                    mk_hl = P.emit(v, lambda e: e.tensor_sub(
                                        den4l2[:], den_sb[:], den4h2[:]),
                                        incs=[(cV, 1)])
                                    P.emit(t, lambda e: e.matmul(
                                        ps_bc2[:], bcsel[:], den4h2[:],
                                        start=True, stop=False),
                                        waits=[(cV, mk_hl[1])])
                                    mk_bc = P.emit(t, lambda e: e.matmul(
                                        ps_bc2[:], bcsel[:], den4l2[:],
                                        start=False, stop=True), incs=[(cT, 1)])
                                    mk_bce = P.emit(v, lambda e: e.tensor_copy(
                                        bc_sb[:], ps_bc2[:]),
                                        waits=[(cT, mk_bc[1])], incs=[(cV, 1)])
                                    mk_pv = P.emit(v, lambda e, gi=gi: e.tensor_mul(
                                        outcT[:, gi * 512:gi * 512 + 512],
                                        ps_pv[:], bc_sb[:]),
                                        waits=[(cV, mk_bce[1]), (cT, cT.v)],
                                        incs=[(cV, 1)])
                                    bnd["fden"] = mk_denevac
                                    bnd["fpv"] = mk_pv
                                return emit_chain
                            chain_pend = make_chain()
                        def make_woln(tc=tc):
                            def piece(i4):
                                nonlocal mk_x1st
                                mk_xo = P.emit(y, lambda e, tc=tc, i4=i4: e.dma_start(
                                    out=xo_t[:],
                                    in_=dxo[tc * 512 + i4 * 128:
                                            tc * 512 + i4 * 128 + 128, :]),
                                    waits=[(cV, cV.v)], incs=[(qXO, 16)])
                                for fh in range(2):
                                    pdst = ps_wo if fh == 0 else ps_bc2
                                    for cg in range(4):
                                        w = []
                                        if fh == 0 and cg == 0:
                                            w = [(cV, cV.v), (qWo, mk_wo[1]),
                                                 (cA, cA.v)]
                                        P.emit(t, lambda e, cg=cg, i4=i4, fh=fh, pdst=pdst:
                                               e.matmul(
                                                   pdst[:, 0:512],
                                                   outcT[:, cg * 512 + i4 * 128:
                                                         cg * 512 + i4 * 128 + 128],
                                                   woef_sb[:, cg * D + fh * 512:
                                                           cg * D + fh * 512 + 512],
                                                   start=(cg == 0), stop=(cg == 3)),
                                               waits=w,
                                               incs=([(cT, 1)] if (cg == 3 and fh == 1)
                                                     else []))
                                mk_y = (cT, cT.v)
                                P.emit(v, lambda e: e.scalar_tensor_tensor(
                                    r_t[:, 0:512], ps_wo[:, 0:512], 1.0, xo_t[:, 0:512],
                                    ALU.mult, ALU.add, accum_out=sm[:, 0:1]),
                                    waits=[mk_y, (qXO, mk_xo[1])])
                                mk_r = P.emit(v, lambda e: e.scalar_tensor_tensor(
                                    r_t[:, 512:1024], ps_bc2[:, 0:512], 1.0,
                                    xo_t[:, 512:1024], ALU.mult, ALU.add,
                                    accum_out=sm[:, 1:2]), incs=[(cV, 1)])
                                P.emit(v, lambda e: e.scalar_tensor_tensor(
                                    xo_t[:], r_t[:], 1.0, r_t[:], ALU.mult, ALU.mult,
                                    accum_out=sm[:, 2:3]))
                                P.emit(v, lambda e: e.tensor_add(
                                    sm[:, 0:1], sm[:, 0:1], sm[:, 1:2]))
                                P.emit(v, lambda e: e.tensor_scalar_mul(
                                    sm[:, 0:1], sm[:, 0:1], 1.0 / 1024.0))
                                P.emit(v, lambda e: e.tensor_scalar_mul(
                                    sm[:, 2:3], sm[:, 2:3], 1.0 / 1024.0))
                                P.emit(v, lambda e: e.tensor_mul(
                                    sm[:, 3:4], sm[:, 0:1], sm[:, 0:1]))
                                P.emit(v, lambda e: e.tensor_sub(
                                    sm[:, 2:3], sm[:, 2:3], sm[:, 3:4]))
                                mk_v = P.emit(v, lambda e: e.tensor_scalar_add(
                                    sm[:, 2:3], sm[:, 2:3], 1e-5), incs=[(cV, 1)])
                                P.emit(s, lambda e: e.activation(
                                    sm[:, 4:5], sm[:, 2:3], AF.Ln), waits=[(cV, mk_v[1])])
                                mke = P.emit(s, lambda e: e.activation(
                                    sm[:, 5:6], sm[:, 4:5], AF.Exp, scale=-0.5),
                                    incs=[(cA, 1)])
                                P.emit(v, lambda e: e.tensor_mul(
                                    sm[:, 6:7], sm[:, 5:6], sm[:, 5:6]),
                                    waits=[(cA, mke[1])])
                                P.emit(v, lambda e: e.tensor_mul(
                                    sm[:, 6:7], sm[:, 6:7], sm[:, 2:3]))
                                P.emit(v, lambda e: e.tensor_scalar(
                                    sm[:, 6:7], sm[:, 6:7], -0.5, 1.5, ALU.mult, ALU.add))
                                P.emit(v, lambda e: e.tensor_mul(
                                    sm[:, 5:6], sm[:, 5:6], sm[:, 6:7]))
                                w = [(qX1, mk_x1st[1])] if mk_x1st is not None else []
                                P.emit(v, lambda e: e.tensor_scalar(
                                    x1_t[:], r_t[:], sm[:, 0:1], None, ALU.subtract),
                                    waits=w)
                                mk_x1 = P.emit(v, lambda e: e.tensor_scalar(
                                    x1_t[:], x1_t[:], sm[:, 5:6], None, ALU.mult),
                                    incs=[(cV, 1)])
                                mk_x1st = P.emit(y, lambda e, tc=tc, i4=i4: e.dma_start(
                                    out=sx1[tc * 512 + i4 * 128:
                                            tc * 512 + i4 * 128 + 128, :],
                                    in_=x1_t[:]), waits=[(cV, mk_x1[1])], incs=[(qX1, 16)])
                            return [lambda i4=i4: piece(i4) for i4 in range(4)]
                        woln_pend = make_woln()
                        if tc == 3:
                            chain_pend()
                            chain_pend = None
                            while woln_pend:
                                woln_pend.pop(0)()

                    with nc.Block() as blk:
                        P.replay(blk)

        s3 = ExitStack()
        with s3:
            b3 = lambda name, shape, dt: s3.enter_context(nc.sbuf_tensor(name, shape, dt))
            x1f = b3("x1f", [128, 4 * D], F32)
            xq = b3("xq", [128, 4 * D], BF16)
            xqT = b3("xqT", [128, 4 * D], BF16)
            wgbuf = [b3("wgbuf0", [128, 8 * 512], BF16),
                     b3("wgbuf1", [128, 8 * 512], BF16)]
            wubuf = [b3("wubuf0", [128, 8 * 512], BF16),
                     b3("wubuf1", [128, 8 * 512], BF16)]
            wdbuf = [b3("wdbuf0", [128, 1024], BF16),
                     b3("wdbuf1", [128, 1024], BF16)]
            gate_s2 = [b3("gate_s0", [128, 512], F32),
                       b3("gate_s1", [128, 512], F32)]
            hbuf = b3("hbuf", [128, 4 * INNER], F32)
            hq = b3("hq", [128, INNER], BF16)
            hqT = b3("hqT", [128, 4 * INNER], BF16)
            r2f = b3("r2f", [128, 4 * D], F32)
            y_t = b3("y_t", [128, D], F32)
            xsc = b3("xsc", [128, 32], F32)
            xsc2 = b3("xsc2", [128, 8], F32)

            ps3 = ExitStack()
            with ps3:
                pb = lambda name, shape, dt: ps3.enter_context(nc.psum_tensor(name, shape, dt))
                ps_f = pb("ps_f", [128, 3072], F32)
                ps_tf = [pb("ps_tf0", [128, 1024], BF16),
                         pb("ps_tf1", [128, 1024], BF16)]

                tfe = [None, None]

                def ftr_batch(srcs, dst_fn, extra_wait, bank, eng=None):
                    for k in range(4):
                        w = []
                        if k == 0:
                            w += extra_wait
                            if tfe[bank] is not None:
                                w.append(tfe[bank])
                        P.emit(t, lambda e, k=k, bank=bank, src=srcs[k]: e.transpose(
                            ps_tf[bank][:, k * 128:(k + 1) * 128], src, ident[:]),
                            waits=w, incs=([(cT, 1)] if k == 3 else []))
                    e2, ctr = (v, cV) if bank == 0 else (s, cA)
                    if eng == "s":
                        e2, ctr = s, cA
                    mk = P.emit(e2, lambda e, bank=bank: dst_fn(e, ps_tf[bank]),
                                waits=[(cT, cT.v)], incs=[(ctr, 1)])
                    tfe[bank] = mk
                    return mk

                mk_wgu = [None, None]
                mk_wdu = [None, None]
                mk_yst = None
                mk_sil2 = [None, None]
                mk_h2 = [None, None]
                mk_dev_last = [None]

                def emit_xprep(tc):
                    w_ld = []
                    if tc == 0:
                        w_ld += [(qX1, qX1.v), (qP[0], qP[0].v), (qP[1], qP[1].v)]
                    if mk_dev_last[0] is not None:
                        w_ld.append(mk_dev_last[0])
                    mk_ld = P.emit(y, lambda e, tc=tc: e.dma_start(
                        out=x1f.rearrange("p (n f) -> p n f", n=4),
                        in_=sx1[tc * 512:(tc + 1) * 512, :]
                        .rearrange("(n p) f -> p n f", p=128)),
                        waits=w_ld, incs=[(qXL, 16)])
                    for i4 in range(4):
                        ti = tc * 4 + i4
                        c8 = 8 * i4
                        P.emit(v, lambda e, i4=i4, c8=c8: e.tensor_reduce(
                            xsc[:, c8:c8 + 1], x1f[:, i4 * D:(i4 + 1) * D],
                            AX, ALU.max, apply_absolute_value=True),
                            waits=[(qXL, mk_ld[1]), (cT, cT.v)])
                        P.emit(v, lambda e, c8=c8: e.tensor_scalar_max(
                            xsc[:, c8:c8 + 1], xsc[:, c8:c8 + 1], 1e-5))
                        P.emit(v, lambda e, c8=c8: e.reciprocal(
                            xsc[:, c8 + 1:c8 + 2], xsc[:, c8:c8 + 1]))
                        P.emit(v, lambda e, c8=c8: e.tensor_scalar_mul(
                            xsc[:, c8 + 1:c8 + 2], xsc[:, c8 + 1:c8 + 2], 127.0))
                        P.emit(v, lambda e, c8=c8: e.tensor_scalar_mul(
                            xsc[:, c8 + 2:c8 + 3], xsc[:, c8:c8 + 1], 1.0 / 127.0))
                        P.emit(v, lambda e, c8=c8: e.tensor_mul(
                            xsc[:, c8 + 3:c8 + 4], swb[:, 0:1], xsc[:, c8 + 2:c8 + 3]))
                        mk_scl = P.emit(v, lambda e, c8=c8: e.tensor_mul(
                            xsc[:, c8 + 4:c8 + 5], swb[:, 1:2], xsc[:, c8 + 2:c8 + 3]),
                            incs=[(cV, 1)])
                        P.emit(v, lambda e, i4=i4, c8=c8: e.tensor_scalar(
                            hbuf[:, 0:D], x1f[:, i4 * D:(i4 + 1) * D],
                            xsc[:, c8 + 1:c8 + 2], MAGIC, ALU.mult, ALU.add))
                        mk_xq = P.emit(v, lambda e, i4=i4: e.tensor_scalar(
                            xq[:, i4 * D:(i4 + 1) * D], hbuf[:, 0:D],
                            MAGIC, 127.0, ALU.subtract, ALU.min),
                            incs=[(cV, 1)])
                        for b in range(2):
                            ew = [(cV, mk_xq[1])] if b == 0 else []
                            if tc == 0 and i4 == 0 and b == 0:
                                ew.append((cA, cA.v))
                            srcs = [xq[:, i4 * D + (4 * b + k) * 128:
                                       i4 * D + (4 * b + k) * 128 + 128]
                                    for k in range(4)]
                            ftr_batch(srcs,
                                      (lambda i4, b: lambda e, psl: _cp(e,
                                          xqT[:, i4 * D + 4 * b * 128:
                                              i4 * D + 4 * b * 128 + 512],
                                          psl[:, 0:512]))(i4, b),
                                      ew, b)
                    return mk_scl

                mk_tr4 = [None, None, None, None]
                if STAGE >= 5:
                    mk_scl = emit_xprep(0)
                for tc in range(4 if STAGE >= 5 else 0):
                    for io in range(8):
                        bi = io % 2
                        w = []
                        if mk_wgu[bi] is not None:
                            w.append(mk_wgu[bi])
                        mk_g = P.emit(y, lambda e, io=io, bi=bi: e.dma_start(
                            out=wgbuf[bi].rearrange("p (c n) -> p c n", c=8)
                            [:, 0:4, :],
                            in_=sg.rearrange("(c p) n -> p c n", p=128)
                            [:, 0:4, io * 512:(io + 1) * 512]),
                            waits=w, incs=[(qGU[bi], 16)])
                        mk_g2 = P.emit(y, lambda e, io=io, bi=bi: e.dma_start(
                            out=wgbuf[bi].rearrange("p (c n) -> p c n", c=8)
                            [:, 4:8, :],
                            in_=sg.rearrange("(c p) n -> p c n", p=128)
                            [:, 4:8, io * 512:(io + 1) * 512]),
                            incs=[(qGUb[bi], 16)])
                        mk_u = P.emit(y, lambda e, io=io, bi=bi: e.dma_start(
                            out=wubuf[bi].rearrange("p (c n) -> p c n", c=8)
                            [:, 0:4, :],
                            in_=su.rearrange("(c p) n -> p c n", p=128)
                            [:, 0:4, io * 512:(io + 1) * 512]),
                            incs=[(qGUc[bi], 16)])
                        mk_u2 = P.emit(y, lambda e, io=io, bi=bi: e.dma_start(
                            out=wubuf[bi].rearrange("p (c n) -> p c n", c=8)
                            [:, 4:8, :],
                            in_=su.rearrange("(c p) n -> p c n", p=128)
                            [:, 4:8, io * 512:(io + 1) * 512]),
                            incs=[(qGUd[bi], 16)])
                        for i4 in range(4):
                            c8 = 8 * i4
                            par = (io * 4 + i4) % 2
                            gb = par * 2048
                            ub = 512 + par * 2048
                            for c in range(8):
                                w = []
                                if c == 0:
                                    w = [(qGU[bi], mk_g[1])]
                                    if io == 0 and i4 == 0:
                                        w.append((cA, cA.v))
                                        if mk_dev_last[0] is not None:
                                            w.append(mk_dev_last[0])
                                    if mk_sil2[par] is not None:
                                        w.append(mk_sil2[par])
                                    if mk_h2[par] is not None:
                                        w.append(mk_h2[par])
                                if c == 4 and i4 == 0:
                                    w.append((qGUb[bi], mk_g2[1]))
                                P.emit(t, lambda e, c=c, i4=i4, bi=bi, gb=gb: e.matmul(
                                    ps_f[:, gb:gb + 512],
                                    xqT[:, i4 * D + c * 128:i4 * D + c * 128 + 128],
                                    wgbuf[bi][:, c * 512:(c + 1) * 512],
                                    start=(c == 0), stop=(c == 7)), waits=w)
                            for c in range(8):
                                w = []
                                if c == 0 and i4 == 0:
                                    w.append((qGUc[bi], mk_u[1]))
                                if c == 4 and i4 == 0:
                                    w.append((qGUd[bi], mk_u2[1]))
                                P.emit(t, lambda e, c=c, i4=i4, bi=bi, ub=ub: e.matmul(
                                    ps_f[:, ub:ub + 512],
                                    xqT[:, i4 * D + c * 128:i4 * D + c * 128 + 128],
                                    wubuf[bi][:, c * 512:(c + 1) * 512],
                                    start=(c == 0), stop=(c == 7)), waits=w,
                                    incs=([(cT, 1)] if c == 7 else []))
                            mk_gu = (cT, cT.v)
                            mk_sil2[par] = P.emit(s, lambda e, c8=c8, gb=gb, par=par:
                                e.activation(
                                    gate_s2[par][:], ps_f[:, gb:gb + 512], AF.Silu,
                                    scale=xsc[:, c8 + 3:c8 + 4]),
                                waits=[mk_gu, (cV, mk_scl[1])], incs=[(cA, 1)])
                            mk_h2[par] = P.emit(v, lambda e, i4=i4, io=io, c8=c8,
                                                ub=ub, par=par:
                                          e.scalar_tensor_tensor(
                                              hbuf[:, i4 * INNER + io * 512:
                                                   i4 * INNER + io * 512 + 512],
                                              ps_f[:, ub:ub + 512],
                                              xsc[:, c8 + 4:c8 + 5], gate_s2[par][:],
                                              ALU.mult, ALU.mult),
                                          waits=[(cA, mk_sil2[par][1])], incs=[(cV, 1)])
                            if io == 0:
                                P.emit(v, lambda e, i4=i4, io=io, c8=c8:
                                       e.tensor_reduce(
                                           xsc[:, c8 + 5:c8 + 6],
                                           hbuf[:, i4 * INNER + io * 512:
                                                i4 * INNER + io * 512 + 512],
                                           AX, ALU.max, apply_absolute_value=True))
                            else:
                                P.emit(v, lambda e, i4=i4, io=io: e.tensor_reduce(
                                    sm[:, 9:10],
                                    hbuf[:, i4 * INNER + io * 512:
                                         i4 * INNER + io * 512 + 512],
                                    AX, ALU.max, apply_absolute_value=True))
                                P.emit(v, lambda e, c8=c8: e.tensor_max(
                                    xsc[:, c8 + 5:c8 + 6], xsc[:, c8 + 5:c8 + 6],
                                    sm[:, 9:10]))
                        mk_wgu[bi] = (cT, cT.v)
                    for i4 in range(4):
                        c8 = 8 * i4
                        P.emit(v, lambda e, c8=c8: e.tensor_scalar_max(
                            xsc[:, c8 + 5:c8 + 6], xsc[:, c8 + 5:c8 + 6], 1e-5))
                        P.emit(v, lambda e, c8=c8: e.reciprocal(
                            xsc[:, c8 + 6:c8 + 7], xsc[:, c8 + 5:c8 + 6]))
                        P.emit(v, lambda e, c8=c8: e.tensor_scalar_mul(
                            xsc[:, c8 + 6:c8 + 7], xsc[:, c8 + 6:c8 + 7], 127.0))
                        P.emit(v, lambda e, c8=c8: e.tensor_scalar_mul(
                            xsc[:, c8 + 7:c8 + 8], xsc[:, c8 + 5:c8 + 6],
                            1.0 / 127.0), incs=[(cV, 1)])
                    mk_dev = None
                    for fh in range(2):
                        for ic in range(16):
                            bi = ic % 2
                            if fh == 0 and ic % 2 == 0:
                                b8 = ic // 2
                                for i4 in range(4):
                                    c8 = 8 * i4
                                    hs = i4 * INNER + b8 * 512
                                    P.emit(v, lambda e, hs=hs, c8=c8: e.tensor_scalar(
                                        hbuf[:, hs:hs + 512], hbuf[:, hs:hs + 512],
                                        xsc[:, c8 + 6:c8 + 7], MAGIC,
                                        ALU.mult, ALU.add))
                                    mk_hq = P.emit(v, lambda e, i4=i4, hs=hs:
                                                   e.tensor_scalar(
                                                       hq[:, i4 * 512:i4 * 512 + 512],
                                                       hbuf[:, hs:hs + 512],
                                                       MAGIC, 127.0,
                                                       ALU.subtract, ALU.min),
                                                   waits=([mk_tr4[i4]]
                                                          if mk_tr4[i4] else []),
                                                   incs=[(cV, 1)])
                                    srcs = [hq[:, i4 * 512 + k * 128:
                                               i4 * 512 + k * 128 + 128]
                                            for k in range(4)]
                                    ftr_batch(srcs,
                                              (lambda i4, b8: lambda e, psl: _cp(e,
                                                  hqT[:, i4 * INNER + b8 * 512:
                                                      i4 * INNER + b8 * 512 + 512],
                                                  psl[:, 0:512]))(i4, b8),
                                              [(cV, mk_hq[1])], i4 % 2,
                                              eng="s")
                                    mk_tr4[i4] = (cT, cT.v)
                            w = []
                            if mk_wdu[bi] is not None:
                                w.append(mk_wdu[bi])
                            mk_wd = P.emit(y if bi == 0 else "scalar",
                                           lambda e, ic=ic, fh=fh, bi=bi: e.dma_start(
                                out=wdbuf[bi].rearrange("p (a n) -> p a n", a=2),
                                in_=sd[ic * 256:(ic + 1) * 256,
                                       fh * 512:(fh + 1) * 512]
                                .rearrange("(a p) n -> p a n", p=128)),
                                waits=w, incs=[(qWD[bi], 16)], dma=True)
                            for sub in range(2):
                                io = ic * 2 + sub
                                for i4 in range(4):
                                    w = ([(qWD[bi], mk_wd[1])]
                                         if (i4 == 0 and sub == 0) else [])
                                    if (i4 == 0 and sub == 0 and fh == 0
                                            and ic % 2 == 0):
                                        w.append((cV, cV.v))
                                        w.append((cA, cA.v))
                                    if io == 0 and i4 == 0:
                                        if fh == 1:
                                            w.append((cV, cV.v))
                                            w.append((cA, cA.v))
                                        if mk_dev is not None:
                                            w.append(mk_dev)
                                    P.emit(t, lambda e, i4=i4, io=io, bi=bi, sub=sub:
                                           e.matmul(
                                               ps_f[:, ((2 + i4) % 4) * 512:
                                                    ((2 + i4) % 4) * 512 + 512],
                                               hqT[:, i4 * INNER + io * 128:
                                                   i4 * INNER + io * 128 + 128],
                                               wdbuf[bi][:, sub * 512:sub * 512 + 512],
                                               start=(io == 0), stop=(io == 31)),
                                           waits=w,
                                           incs=([(cT, 1)] if (i4 == 3 and sub == 1)
                                                 else []))
                            mk_wdu[bi] = (cT, cT.v)
                        mk_dn = (cT, cT.v)
                        for i4 in range(4):
                            c8 = 8 * i4
                            P.emit(v, lambda e, c8=c8: e.tensor_mul(
                                sm[:, 8:9], swb[:, 2:3], xsc[:, c8 + 7:c8 + 8]),
                                waits=[mk_dn] if i4 == 0 else [])
                            mk_dev = P.emit(v, lambda e, i4=i4, fh=fh:
                                            e.scalar_tensor_tensor(
                                                r2f[:, i4 * D + fh * 512:
                                                    i4 * D + fh * 512 + 512],
                                                ps_f[:, ((2 + i4) % 4) * 512:
                                                     ((2 + i4) % 4) * 512 + 512],
                                                sm[:, 8:9],
                                                x1f[:, i4 * D + fh * 512:
                                                    i4 * D + fh * 512 + 512],
                                                ALU.mult, ALU.add,
                                                accum_out=xsc2[:, 2 * i4 + fh:
                                                               2 * i4 + fh + 1]),
                                            incs=[(cV, 1)])
                    mk_dev_last[0] = mk_dev
                    if tc < 3:
                        mk_scl = emit_xprep(tc + 1)
                    for i4 in range(4):
                        P.emit(v, lambda e, i4=i4: e.scalar_tensor_tensor(
                            hbuf[:, 0:D], r2f[:, i4 * D:(i4 + 1) * D], 1.0,
                            r2f[:, i4 * D:(i4 + 1) * D], ALU.mult, ALU.mult,
                            accum_out=sm[:, 2:3]))
                        P.emit(v, lambda e, i4=i4: e.tensor_add(
                            sm[:, 0:1], xsc2[:, 2 * i4:2 * i4 + 1],
                            xsc2[:, 2 * i4 + 1:2 * i4 + 2]))
                        P.emit(v, lambda e: e.tensor_scalar_mul(
                            sm[:, 0:1], sm[:, 0:1], 1.0 / 1024.0))
                        P.emit(v, lambda e: e.tensor_scalar_mul(
                            sm[:, 2:3], sm[:, 2:3], 1.0 / 1024.0))
                        P.emit(v, lambda e: e.tensor_mul(
                            sm[:, 3:4], sm[:, 0:1], sm[:, 0:1]))
                        P.emit(v, lambda e: e.tensor_sub(
                            sm[:, 2:3], sm[:, 2:3], sm[:, 3:4]))
                        mk_v = P.emit(v, lambda e: e.tensor_scalar_add(
                            sm[:, 2:3], sm[:, 2:3], 1e-5), incs=[(cV, 1)])
                        P.emit(s, lambda e: e.activation(
                            sm[:, 4:5], sm[:, 2:3], AF.Ln), waits=[(cV, mk_v[1])])
                        mke = P.emit(s, lambda e: e.activation(
                            sm[:, 5:6], sm[:, 4:5], AF.Exp, scale=-0.5), incs=[(cA, 1)])
                        P.emit(v, lambda e: e.tensor_mul(
                            sm[:, 6:7], sm[:, 5:6], sm[:, 5:6]), waits=[(cA, mke[1])])
                        P.emit(v, lambda e: e.tensor_mul(
                            sm[:, 6:7], sm[:, 6:7], sm[:, 2:3]))
                        P.emit(v, lambda e: e.tensor_scalar(
                            sm[:, 6:7], sm[:, 6:7], -0.5, 1.5, ALU.mult, ALU.add))
                        P.emit(v, lambda e: e.tensor_mul(
                            sm[:, 5:6], sm[:, 5:6], sm[:, 6:7]))
                        w = [(qY, mk_yst[1])] if mk_yst is not None else []
                        P.emit(v, lambda e, i4=i4: e.tensor_scalar(
                            y_t[:], r2f[:, i4 * D:(i4 + 1) * D], sm[:, 0:1], None,
                            ALU.subtract), waits=w)
                        mk_yv = P.emit(v, lambda e: e.tensor_scalar(
                            y_t[:], y_t[:], sm[:, 5:6], None, ALU.mult),
                            incs=[(cV, 1)])
                        mk_yst = P.emit(y, lambda e, tc=tc, i4=i4: e.dma_start(
                            out=dy[tc * 512 + i4 * 128:tc * 512 + i4 * 128 + 128, :],
                            in_=y_t[:]), waits=[(cV, mk_yv[1])], incs=[(qY, 16)])

                P.emit(y, None, waits=[(qY, qY.v)])

                with nc.Block() as blk:
                    P.replay(blk)

    return nc


_NC_CACHE = None


def _get_nc():
    global _NC_CACHE
    if _NC_CACHE is None:
        _NC_CACHE = build_nc()
    return _NC_CACHE


def kernel(L0_attn_x, L0_attn_Wq, L0_attn_Wkv, L0_attn_Wk, L0_attn_Wv, L0_attn_Wo,
           L0_attn_q_g, L0_attn_k_g, L0_ffn_w_gate, L0_ffn_w_up, L0_ffn_w_down):
    x = np.ascontiguousarray(np.asarray(L0_attn_x, dtype=np.float32))
    wq = np.ascontiguousarray(np.asarray(L0_attn_Wq, dtype=np.float32))
    wkv = np.ascontiguousarray(np.asarray(L0_attn_Wkv, dtype=np.float32))
    wk = np.ascontiguousarray(np.asarray(L0_attn_Wk, dtype=np.float32))
    wv = np.ascontiguousarray(np.asarray(L0_attn_Wv, dtype=np.float32))
    wo = np.asarray(L0_attn_Wo, dtype=np.float32)
    woef = np.ascontiguousarray(
        wo.reshape(16, 64, D)[:, :32, :].reshape(512, D))
    qg = np.asarray(L0_attn_q_g, dtype=np.float32)
    kg = np.asarray(L0_attn_k_g, dtype=np.float32)
    qg4r = np.ascontiguousarray(np.tile(qg[:32], 4).reshape(128, 1))
    kg4 = np.ascontiguousarray(np.tile(kg, 4).reshape(128, 1))
    wgT = np.ascontiguousarray(np.asarray(L0_ffn_w_gate, dtype=np.float32).T)
    wuT = np.ascontiguousarray(np.asarray(L0_ffn_w_up, dtype=np.float32).T)
    wdT = np.ascontiguousarray(np.asarray(L0_ffn_w_down, dtype=np.float32).T)

    own_idx = {h: np.concatenate([np.arange(128 * (2 * j + h), 128 * (2 * j + h) + 128)
                                  for j in range(16)]) for h in range(2)}

    in_maps = []
    for core in range(8):
        b, h = core // 2, core % 2
        xo = np.ascontiguousarray(x[b][own_idx[h]])
        in_maps.append({
            "x": x[b], "xo": xo, "wq": wq, "wkv": wkv, "wk": wk, "wv": wv,
            "woef": woef, "wgT": wgT, "wuT": wuT, "wdT": wdT,
            "qg4r": qg4r, "kg4": kg4,
            "hflag": np.full((128, 1), float(h), dtype=np.float32),
        })

    nc = _get_nc()
    res = run_bass_kernel_spmd(nc, in_maps, core_ids=list(range(8)))

    out = np.empty((4, T, D), dtype=np.float32)
    for core in range(8):
        b, h = core // 2, core % 2
        out[b][own_idx[h]] = res.results[core]["y"]
    return out


# version 22 (final)
# speedup vs baseline: 1.0362x; 1.0362x over previous
import numpy as np
import ml_dtypes
from contextlib import ExitStack

import os
import concourse.bass as bass
import concourse.mybir as mybir
from concourse.bass_utils import run_bass_kernel_spmd

F32 = mybir.dt.float32
BF16 = mybir.dt.bfloat16
AF = mybir.ActivationFunctionType
ALU = mybir.AluOpType
AX = mybir.AxisListType.X

MAGIC = float(np.float32(1.5 * 2 ** 23))
STAGE = int(os.environ.get("KSTAGE", "5"))
NB5 = int(os.environ.get("KNB5", "8"))
NHC = int(os.environ.get("KNHC", "4"))
HCPART = int(os.environ.get("KHCPART", "4"))
T = 4096
D = 1024
OWN = 2048
INNER = 4096


def _cp(e, dst, src):
    if hasattr(e, "tensor_copy"):
        return e.tensor_copy(dst, src)
    return e.copy(dst, src)


class Ctr:
    __slots__ = ("h", "v")

    def __init__(self, h):
        self.h = h
        self.v = 0


class Prog:

    ENGS = ("tensor", "vector", "scalar", "gpsimd", "sync")

    def __init__(self):
        self.q = {e: [] for e in self.ENGS}

    def set_auto(self, mapping):
        self.auto = mapping

    def emit(self, eng, fn, waits=(), incs=(), dma=False):
        w = [(c.h, v) for (c, v) in waits]
        incs = list(incs)
        ac = getattr(self, "auto", {}).get(eng)
        if ac is not None and fn is not None and not dma:
            if ac.v > 0:
                w.append((ac.h, ac.v))
            if not any(c is ac for c, _ in incs):
                incs.append((ac, 1))
        ii = []
        mark = None
        for (c, n) in incs:
            c.v += n
            ii.append((c.h, n))
            mark = (c, c.v)
        self.q[eng].append((fn, w, ii))
        return mark

    def replay(self, blk):
        q = self.q
        self.q = {e: [] for e in self.ENGS}

        def runner(ops):
            def run(e):
                for fn, waits, incs in ops:
                    for h, v in waits:
                        e.wait_ge(h, v)
                    if fn is None:
                        assert not incs, "fn=None cannot carry incs"
                        continue
                    inst = fn(e)
                    for h, n in incs:
                        inst.then_inc(h, n)
            return run

        for eng in self.ENGS:
            getattr(blk, eng)(runner(q[eng]))


def build_nc():
    nc = bass.Bass("TRN2", target_bir_lowering=False)

    dx = nc.dram_tensor("x", [T, D], F32, kind="ExternalInput").ap()
    dxo = nc.dram_tensor("xo", [OWN, D], F32, kind="ExternalInput").ap()
    dwq = nc.dram_tensor("wq", [D, D], F32, kind="ExternalInput").ap()
    dwkv = nc.dram_tensor("wkv", [D, 128], F32, kind="ExternalInput").ap()
    dwk = nc.dram_tensor("wk", [128, 512], F32, kind="ExternalInput").ap()
    dwv = nc.dram_tensor("wv", [128, 512], F32, kind="ExternalInput").ap()
    dwo = nc.dram_tensor("woef", [512, D], F32, kind="ExternalInput").ap()
    dwg = nc.dram_tensor("wgT", [D, INNER], F32, kind="ExternalInput").ap()
    dwu = nc.dram_tensor("wuT", [D, INNER], F32, kind="ExternalInput").ap()
    dwd = nc.dram_tensor("wdT", [INNER, D], F32, kind="ExternalInput").ap()
    dqg = nc.dram_tensor("qg4r", [128, 1], F32, kind="ExternalInput").ap()
    dkg = nc.dram_tensor("kg4", [128, 1], F32, kind="ExternalInput").ap()
    dhf = nc.dram_tensor("hflag", [128, 1], F32, kind="ExternalInput").ap()
    dy = nc.dram_tensor("y", [OWN, D], F32, kind="ExternalOutput").ap()

    sg = nc.dram_tensor("scr_g", [D, INNER], BF16, kind="Internal").ap()
    su = nc.dram_tensor("scr_u", [D, INNER], BF16, kind="Internal").ap()
    sd = nc.dram_tensor("scr_d", [INNER, D], BF16, kind="Internal").ap()
    sx1 = nc.dram_tensor("scr_x1", [OWN, D], F32, kind="Internal").ap()

    P = Prog()
    t, v, s, g, y = "tensor", "vector", "scalar", "gpsimd", "sync"

    es = ExitStack()
    with es:
        sb = lambda name, shape, dt: es.enter_context(nc.sbuf_tensor(name, shape, dt))

        ident = sb("ident", [128, 128], BF16)
        maskD = sb("maskD", [128, 256], BF16)
        mtmp = sb("mtmp", [128, 256], BF16)
        ind4T = sb("ind4T", [128, 4], BF16)
        ind4 = sb("ind4", [4, 128], BF16)
        dencol = sb("dencol", [128, 32], BF16)
        qsel01 = sb("qsel01", [128, 4], BF16)
        qsel23 = sb("qsel23", [128, 4], BF16)
        onesf = sb("onesf", [128, 128], F32)
        bcsel = sb("bcsel", [128, 128], BF16)
        qg8 = sb("qg8", [128, 1], F32)
        kg4 = sb("kg4_sb", [128, 1], F32)
        hflag = sb("hflag_sb", [128, 1], F32)
        c127 = sb("c127", [128, 1], F32)
        swb = sb("swb", [128, 4], F32)
        sclw = sb("sclw", [128, 1], F32)
        absacc = sb("absacc", [128, 1], F32)
        negMb = sb("negMb", [128, 1], F32)
        sm = sb("sm", [128, 16], F32)
        sm4 = sb("sm4", [4, 2048], F32)

        cT = Ctr(es.enter_context(nc.semaphore("cT")))
        cA = Ctr(es.enter_context(nc.semaphore("cA")))
        cV = Ctr(es.enter_context(nc.semaphore("cV")))
        cG = Ctr(es.enter_context(nc.semaphore("cG")))
        mksem = lambda n: Ctr(es.enter_context(nc.semaphore(n)))
        qSm = mksem("qSm"); qW = mksem("qW"); qWo = mksem("qWo")
        qF = [mksem("qF0"), mksem("qF1")]
        qP = [mksem("qP0"), mksem("qP1")]
        qX2 = [mksem("qXa"), mksem("qXb")]; qD = mksem("qD"); qM = mksem("qM")
        qXO = mksem("qXO"); qX1 = mksem("qX1"); qXL = mksem("qXL")
        qGU = [mksem("qGU0"), mksem("qGU1")]
        qGUb = [mksem("qGUb0"), mksem("qGUb1")]
        qGUc = [mksem("qGUc0"), mksem("qGUc1")]
        qGUd = [mksem("qGUd0"), mksem("qGUd1")]
        qWD = [mksem("qWD0"), mksem("qWD1")]
        qWD2 = [mksem("qWD2a"), mksem("qWD2b")]
        qY = mksem("qY")
        P.set_auto({"vector": cV, "scalar": cA, "gpsimd": cG})

        at = ExitStack()
        with at:
            ab = lambda name, shape, dt: at.enter_context(nc.sbuf_tensor(name, shape, dt))
            kvT = ab("kvT", [128, T], BF16)
            khat = ab("khat", [128, 4 * T], BF16)
            vtok = ab("vtok", [128, 32 * 512], BF16)
            qn = ab("qn", [128, 4 * OWN], BF16)
            kmax2 = ab("kmax2", [4, 4], F32)
            qmax2 = ab("qmax2", [4, 4], F32)

            s1 = ExitStack()
            with s1:
                b1 = lambda name, shape, dt: s1.enter_context(nc.sbuf_tensor(name, shape, dt))
                wq_sb = b1("wq_sb", [128, 8 * D], BF16)
                wkv_sb = b1("wkv_sb", [128, 8 * 128], BF16)
                wk_sb = b1("wk_sb", [128, 512], BF16)
                wv_sb = b1("wv_sb", [128, 512], BF16)
                xst2 = [b1("xst0", [128, 4 * D], BF16),
                        b1("xst1", [128, 4 * D], BF16)]
                xTc2 = [b1("xTc0", [128, 8 * 512], BF16),
                        b1("xTc1", [128, 8 * 512], BF16)]
                krbuf = b1("krbuf", [128, 512], BF16)
                kr2buf = b1("kr2buf", [128, 512], BF16)
                krall = b1("krall", [128, 4 * 512], BF16)
                vtbuf = b1("vtbuf", [128, 512], BF16)
                qfull = b1("qfull", [128, 512], BF16)
                q2buf = b1("q2buf", [128, 2 * 512], BF16)
                rkc = b1("rkc", [4, 512], F32)
                rkh = b1("rkh", [4, 512], BF16)
                rkl = b1("rkl", [4, 512], BF16)
                wfbuf = [b1("wfbuf0", [128, INNER], F32),
                         b1("wfbuf1", [128, INNER], F32)]
                wqbuf_s = b1("wqbuf_s", [128, INNER], BF16)
                wqbuf = [wqbuf_s, wqbuf_s]

                ps1 = ExitStack()
                with ps1:
                    pb = lambda name, shape, dt: ps1.enter_context(nc.psum_tensor(name, shape, dt))
                    ps_a = pb("ps_a", [128, 512], F32)
                    ps_b = pb("ps_b", [128, 512], F32)
                    ps_ss = pb("ps_ss", [4, 512], F32)
                    ps_bc = pb("ps_bc", [128, 512], F32)
                    ps_tr = [pb("ps_tr0", [128, 1024], BF16),
                             pb("ps_tr1", [128, 1024], BF16)]
                    ps_vt = pb("ps_vt", [128, 512], BF16)
                    ps_sw = pb("ps_sw", [128, 1], F32)

                    P.emit(y, lambda e: e.dma_start(out=qg8[:], in_=dqg), incs=[(qSm, 16)])
                    P.emit(y, lambda e: e.dma_start(out=kg4[:], in_=dkg), incs=[(qSm, 16)])
                    mk_small = P.emit(y, lambda e: e.dma_start(out=hflag[:], in_=dhf),
                                      incs=[(qSm, 16)])
                    _gprev = None
                    def _gchain(fn):
                        global _unused
                        nonlocal _gprev
                        _gprev = P.emit(g, fn, waits=([_gprev] if _gprev else []), incs=[(cG, 1)])
                    _gchain(lambda e: e.memset(ident[:], 1.0))
                    _gchain(lambda e: e.affine_select(ident[:], ident[:], [[1, 128]], ALU.is_equal, 0.0, base=0, channel_multiplier=-1))
                    _gchain(lambda e: e.memset(mtmp[:], 1.0))
                    _gchain(lambda e: e.affine_select(mtmp[:, 0:128], mtmp[:, 0:128], [[1, 128]], ALU.is_ge, 0.0, base=0, channel_multiplier=-1))
                    _gchain(lambda e: e.memset(mtmp[:, 128:256], 0.0))
                    _gchain(lambda e: e.memset(maskD[:], 1.0))
                    _gchain(lambda e: e.affine_select(maskD[:, 128:256], maskD[:, 128:256], [[1, 128]], ALU.is_ge, 0.0, base=0, channel_multiplier=-1))
                    _gchain(lambda e: e.memset(ind4T[:], 1.0))
                    _gchain(lambda e: e.affine_select(ind4T[:], ind4T[:], [[-32, 4]], ALU.is_ge, 0.0, base=0, channel_multiplier=1))
                    _gchain(lambda e: e.affine_select(ind4T[:], ind4T[:], [[32, 4]], ALU.is_ge, 0.0, base=31, channel_multiplier=-1))
                    _gchain(lambda e: e.memset(ind4[:], 1.0))
                    _gchain(lambda e: e.affine_select(ind4[:], ind4[:], [[1, 128]], ALU.is_ge, 0.0, base=0, channel_multiplier=-32))
                    _gchain(lambda e: e.affine_select(ind4[:], ind4[:], [[-1, 128]], ALU.is_ge, 0.0, base=31, channel_multiplier=32))
                    _gchain(lambda e: e.memset(dencol[:], 0.0))
                    _gchain(lambda e: e.memset(dencol[:, 0:1], 1.0))
                    _gchain(lambda e: e.memset(qsel01[:], 0.0))
                    _gchain(lambda e: e.memset(qsel01[0:64, 0:1], 1.0))
                    _gchain(lambda e: e.memset(qsel01[64:128, 1:2], 1.0))
                    _gchain(lambda e: e.memset(qsel23[:], 0.0))
                    _gchain(lambda e: e.memset(qsel23[0:64, 2:3], 1.0))
                    _gchain(lambda e: e.memset(qsel23[64:128, 3:4], 1.0))
                    _gchain(lambda e: e.memset(onesf[:], 1.0))
                    _gchain(lambda e: e.memset(c127[:], 127.0))
                    _gchain(lambda e: e.memset(kmax2[:], 0.0))
                    _gchain(lambda e: e.memset(qmax2[:], 0.0))
                    _gchain(lambda e: e.memset(bcsel[:], 1.0))
                    for _j in range(4):
                        _gchain(lambda e, _j=_j: e.affine_select(
                            bcsel[:, 32 * _j:32 * _j + 32], bcsel[:, 32 * _j:32 * _j + 32],
                            [[0, 32]], ALU.is_equal, 0.0, base=-32 * _j,
                            channel_multiplier=1))
                    mk_const = _gprev
                    P.emit(v, lambda e: e.tensor_sub(maskD[:], maskD[:], mtmp[:]),
                           waits=[(qSm, mk_small[1]), (cG, mk_const[1])])
                    P.emit(v, lambda e: e.scalar_tensor_tensor(
                        maskD[:], maskD[:], hflag[:, 0:1], mtmp[:], ALU.mult, ALU.add))
                    P.emit(v, lambda e: e.tensor_scalar_mul(qg8[:], qg8[:], 0.125),
                           incs=[(cV, 1)])
                    P.emit(g, lambda e: e.dma_start(
                        out=wq_sb.rearrange("p (c n) -> p c n", c=8),
                        in_=dwq.rearrange("(c p) n -> p c n", p=128)), incs=[(qW, 16)], dma=True)
                    P.emit(g, lambda e: e.dma_start(
                        out=wkv_sb.rearrange("p (c n) -> p c n", c=8),
                        in_=dwkv.rearrange("(c p) n -> p c n", p=128)), incs=[(qW, 16)], dma=True)
                    P.emit(g, lambda e: e.dma_start(out=wk_sb[:], in_=dwk), incs=[(qW, 16)], dma=True)
                    mk_wload = P.emit(g, lambda e: e.dma_start(out=wv_sb[:], in_=dwv),
                                      incs=[(qW, 16)], dma=True)

                    def f0_emitter():
                        dmk = [None, None]
                        for wi, (src, dst, nt, ncols) in enumerate(
                                [(dwg, sg, 8, INNER), (dwu, su, 8, INNER), (dwd, sd, 32, D)]
                                if STAGE >= 1 else []):
                            mk_z = P.emit(g, lambda e: e.memset(absacc[:], 0.0),
                                          waits=[(cV, cV.v)], incs=[(cG, 1)])
                            vmk = [None, None]
                            for i in range(nt):
                                bi = i % 2
                                w = [(cG, mk_z[1])] if i < 2 else []
                                if dmk[bi] is not None:
                                    w.append((qP[bi], dmk[bi][1]))
                                if vmk[bi] is not None:
                                    w.append((cV, vmk[bi][1]))
                                mkd = P.emit(y if bi == 0 else "scalar",
                                             lambda e, i=i, bi=bi, src=src, ncols=ncols:
                                             e.dma_start(out=wfbuf[bi][:, 0:ncols],
                                                         in_=src[i * 128:(i + 1) * 128, :]),
                                             waits=w, incs=[(qF[bi], 16)], dma=True)
                                wsm = [(qF[bi], mkd[1])]
                                wsm += [(qP[k], dmk[k][1]) for k in range(2) if dmk[k]]
                                if vmk[0] is not None:
                                    wsm.append(vmk[0])
                                if vmk[1] is not None:
                                    wsm.append(vmk[1])
                                mka = P.emit(s, lambda e, bi=bi, ncols=ncols: e.activation(
                                    wqbuf[bi][:, 0:ncols], wfbuf[bi][:, 0:ncols], AF.Abs,
                                    accum_out=sm[:, 0:1]), waits=wsm,
                                    incs=[(cA, 1)])
                                vmk[bi] = P.emit(v, lambda e: e.tensor_add(
                                    absacc[:], absacc[:], sm[:, 0:1]),
                                    waits=[(cA, mka[1])], incs=[(cV, 1)])
                                yield
                            mkmm = P.emit(t, lambda e: e.matmul(
                                ps_sw[:], onesf[:], absacc[:], start=True, stop=True),
                                waits=[(cV, cV.v)], incs=[(cT, 1)])
                            P.emit(v, lambda e, wi=wi: e.tensor_scalar(
                                swb[:, wi:wi + 1], ps_sw[:], 1.0 / float(INNER * D), 1e-5,
                                ALU.mult, ALU.max), waits=[(cT, mkmm[1])])
                            mksc = P.emit(v, lambda e, wi=wi: e.reciprocal(
                                sclw[:], swb[:, wi:wi + 1]), waits=[(cA, cA.v)],
                                incs=[(cV, 1)])
                            yield
                            for i in range(nt):
                                bi = i % 2
                                w = [(cV, mksc[1])]
                                if dmk[bi] is not None:
                                    w.append((qP[bi], dmk[bi][1]))
                                mkd = P.emit(y if bi == 0 else "scalar",
                                             lambda e, i=i, bi=bi, src=src, ncols=ncols:
                                             e.dma_start(out=wfbuf[bi][:, 0:ncols],
                                                         in_=src[i * 128:(i + 1) * 128, :]),
                                             waits=w, incs=[(qF[bi], 16)], dma=True)
                                P.emit(v, lambda e, bi=bi, ncols=ncols: e.tensor_scalar(
                                    wfbuf[bi][:, 0:ncols], wfbuf[bi][:, 0:ncols],
                                    sclw[:, 0:1], MAGIC, ALU.mult, ALU.add),
                                    waits=[(qF[bi], mkd[1])])
                                P.emit(v, lambda e, bi=bi, ncols=ncols: e.tensor_scalar(
                                    wfbuf[bi][:, 0:ncols], wfbuf[bi][:, 0:ncols],
                                    MAGIC, 1.0, ALU.subtract, ALU.min))
                                wv2 = [(qP[k], dmk[k][1]) for k in range(2) if dmk[k]]
                                vmq = P.emit(v, lambda e, bi=bi, ncols=ncols: e.tensor_scalar_max(
                                    wqbuf[bi][:, 0:ncols], wfbuf[bi][:, 0:ncols], -1.0),
                                    waits=wv2, incs=[(cV, 1)])
                                dmk[bi] = P.emit(y, lambda e, i=i, bi=bi, dst=dst, ncols=ncols:
                                                 e.dma_start(out=dst[i * 128:(i + 1) * 128, :],
                                                             in_=wqbuf[bi][:, 0:ncols]),
                                                 waits=[(cV, vmq[1])], incs=[(qP[bi], 16)])
                                yield
                        mk_f0_v = P.emit(v, lambda e: e.engine_nop(), incs=[(cV, 1)])
                        yield
                    f0_gen = f0_emitter()
                    f0_done = [False]

                    def f0_step(n=1):
                        for _ in range(n):
                            if not f0_done[0]:
                                try:
                                    next(f0_gen)
                                except StopIteration:
                                    f0_done[0] = True

                    tr_evac = [None, None]

                    def tr_batch(srcs, dst_fn, extra_wait, bank):
                        for i4 in range(4):
                            w = []
                            if i4 == 0:
                                w += extra_wait
                                if tr_evac[bank] is not None:
                                    w.append(tr_evac[bank])
                            P.emit(t, lambda e, i4=i4, bank=bank, src=srcs[i4]:
                                   e.transpose(ps_tr[bank][:, i4 * 128:(i4 + 1) * 128],
                                               src, ident[:]),
                                   waits=w, incs=([(cT, 1)] if i4 == 3 else []))
                        eng, ctr = (v, cV) if bank == 0 else (s, cA)
                        mk = P.emit(eng, lambda e, bank=bank: dst_fn(e, ps_tr[bank]),
                                    waits=[(cT, cT.v)], incs=[(ctr, 1)])
                        tr_evac[bank] = mk
                        return mk

                    mk_trdone = [None, None]
                    for c5 in range((8 if STAGE > 2 else NB5) if STAGE >= 2 else 0):
                        s0 = c5 * 512
                        par5 = c5 % 2
                        xst = xst2[par5]
                        xTc = xTc2[par5]
                        w = [(qW, mk_wload[1])] if c5 == 0 else []
                        if c5 < 2:
                            w += [(qP[0], qP[0].v), (qP[1], qP[1].v)]
                        if mk_trdone[par5] is not None:
                            w.append(mk_trdone[par5])
                        mk_x = P.emit(g, lambda e, s0=s0, xst=xst: e.dma_start(
                            out=xst.rearrange("p (n f) -> p n f", n=4),
                            in_=dx[s0:s0 + 512, :].rearrange("(n p) f -> p n f", p=128)),
                            waits=w, incs=[(qX2[par5], 16)], dma=True)
                        for c in range(8):
                            srcs = [xst[:, i4 * D + c * 128:i4 * D + c * 128 + 128]
                                    for i4 in range(4)]
                            ew = [(qX2[par5], mk_x[1])] if c == 0 else []
                            if c == 0 and c5 == 0:
                                ew.append((cG, cG.v))
                            tr_batch(srcs,
                                     (lambda c, xTc=xTc: lambda e, psl: _cp(e,
                                         xTc[:, c * 512:(c + 1) * 512],
                                         psl[:, 0:512]))(c),
                                     ew, c % 2)
                        mk_trdone[par5] = (cT, cT.v)
                        for c in range(8):
                            w = [(cV, cV.v), (cA, cA.v)] if c == 0 else []
                            P.emit(t, lambda e, c=c, xTc=xTc: e.matmul(
                                ps_a[:], wkv_sb[:, c * 128:(c + 1) * 128],
                                xTc[:, c * 512:(c + 1) * 512],
                                start=(c == 0), stop=(c == 7)), waits=w,
                                incs=([(cT, 1)] if c == 7 else []))
                        mk_kv = P.emit(v, lambda e, s0=s0: e.tensor_copy(
                            kvT[:, s0:s0 + 512], ps_a[:]), waits=[(cT, cT.v)],
                            incs=[(cV, 1)])
                        f0_step(1)
                        mk_vread = [mk_kv, None]
                        for hc in range(NHC):
                            co = hc * T + s0
                            pk = ps_b if hc % 2 == 0 else ps_a
                            pvp = ps_a if hc % 2 == 0 else ps_b
                            kb = 1 - (hc % 2)
                            vb = hc % 2
                            w1 = [(cV, mk_kv[1])]
                            if mk_vread[kb] is not None:
                                w1.append(mk_vread[kb])
                            mk1 = P.emit(t, lambda e, hc=hc, s0=s0, pk=pk: e.matmul(
                                pk[:], wk_sb[:, hc * 128:(hc + 1) * 128],
                                kvT[:, s0:s0 + 512], start=True, stop=True),
                                waits=w1, incs=[(cT, 1)])
                            mk_krc = P.emit(v, lambda e, pk=pk: e.tensor_copy(
                                krbuf[:], pk[:]), waits=[(cT, mk1[1])], incs=[(cV, 1)])
                            mk_vread[kb] = mk_krc
                            P.emit(v, lambda e: e.tensor_mul(kr2buf[:], krbuf[:], krbuf[:]))
                            mk2 = P.emit(v, lambda e, hc=hc: e.tensor_copy(
                                krall[:, hc * 512:hc * 512 + 512], krbuf[:]),
                                incs=[(cV, 1)])
                            if HCPART < 1:
                                continue
                            mk3 = P.emit(t, lambda e: e.matmul(
                                ps_ss[:], ind4T[:], kr2buf[:], start=True, stop=True),
                                waits=[(cV, mk2[1])], incs=[(cT, 1)])
                            P.emit(v, lambda e: e.tensor_scalar(
                                sm4[:, 0:512], ps_ss[:], 1.0 / 32.0, 1e-6,
                                ALU.mult, ALU.add), waits=[(cT, mk3[1])], incs=[(cV, 1)])
                            P.emit(s, lambda e: e.activation(
                                sm4[:, 512:1024], sm4[:, 0:512], AF.Ln),
                                waits=[(cV, cV.v)])
                            mke = P.emit(s, lambda e: e.activation(
                                sm4[:, 1024:1536], sm4[:, 512:1024], AF.Exp, scale=-0.5),
                                incs=[(cA, 1)])
                            P.emit(v, lambda e: e.tensor_mul(
                                sm4[:, 512:1024], sm4[:, 1024:1536], sm4[:, 1024:1536]),
                                waits=[(cA, mke[1])])
                            P.emit(v, lambda e: e.tensor_mul(
                                sm4[:, 512:1024], sm4[:, 512:1024], sm4[:, 0:512]))
                            P.emit(v, lambda e: e.tensor_scalar(
                                sm4[:, 512:1024], sm4[:, 512:1024], -0.5, 1.5,
                                ALU.mult, ALU.add))
                            P.emit(v, lambda e: e.tensor_mul(
                                rkc[:], sm4[:, 1024:1536], sm4[:, 512:1024]))
                            P.emit(v, lambda e: e.tensor_copy(rkh[:], rkc[:]))
                            mk4 = P.emit(v, lambda e: e.tensor_sub(
                                rkl[:], rkc[:], rkh[:]), incs=[(cV, 1)])
                            if HCPART < 2:
                                continue
                            P.emit(t, lambda e: e.matmul(
                                ps_bc[:], ind4[:], rkh[:],
                                start=True, stop=False),
                                waits=[(cV, mk4[1])])
                            mk5 = P.emit(t, lambda e: e.matmul(
                                ps_bc[:], ind4[:], rkl[:],
                                start=False, stop=True), incs=[(cT, 1)])
                            P.emit(v, lambda e, co=co, hc=hc: e.scalar_tensor_tensor(
                                khat[:, co:co + 512], krall[:, hc * 512:hc * 512 + 512],
                                kg4[:, 0:1], ps_bc[:], ALU.mult, ALU.mult),
                                waits=[(cT, mk5[1])], incs=[(cV, 1)])
                            if HCPART < 3:
                                continue
                            mk7 = P.emit(v, lambda e, co=co: e.tensor_mul(
                                kr2buf[:], khat[:, co:co + 512], khat[:, co:co + 512]),
                                incs=[(cV, 1)])
                            mk8 = P.emit(t, lambda e: e.matmul(
                                ps_ss[:], ind4T[:], kr2buf[:], start=True, stop=True),
                                waits=[(cV, mk7[1])], incs=[(cT, 1)])
                            P.emit(v, lambda e: e.tensor_reduce(
                                sm4[:, 0:1], ps_ss[:], AX, ALU.max), waits=[(cT, mk8[1])])
                            P.emit(v, lambda e, hc=hc: e.tensor_max(
                                kmax2[:, hc:hc + 1], kmax2[:, hc:hc + 1], sm4[:, 0:1]),
                                incs=[(cV, 1)])
                            if HCPART < 4:
                                continue
                            w9 = []
                            if mk_vread[vb] is not None:
                                w9.append(mk_vread[vb])
                            mk9 = P.emit(t, lambda e, hc=hc, s0=s0, pvp=pvp: e.matmul(
                                pvp[:], wv_sb[:, hc * 128:(hc + 1) * 128],
                                kvT[:, s0:s0 + 512], start=True, stop=True),
                                waits=w9, incs=[(cT, 1)])
                            mk10 = P.emit(v, lambda e, pvp=pvp: e.tensor_copy(
                                vtbuf[:], pvp[:]), waits=[(cT, mk9[1])], incs=[(cV, 1)])
                            mk_vread[vb] = mk10
                            for ss in range(4):
                                w = [(cV, mk10[1])] if ss == 0 else []
                                P.emit(t, lambda e, ss=ss: e.transpose(
                                    ps_vt[:, ss * 128:(ss + 1) * 128],
                                    vtbuf[:, ss * 128:(ss + 1) * 128], ident[:]),
                                    waits=w, incs=([(cT, 1)] if ss == 3 else []))
                            P.emit(v, lambda e, c5=c5, hc=hc: e.tensor_copy(
                                vtok.rearrange("p (a n) -> p a n", n=512)
                                [:, 4 * c5:4 * c5 + 4, hc * 128:hc * 128 + 128],
                                ps_vt.rearrange("p (a n) -> p a n", n=128)),
                                waits=[(cT, cT.v)], incs=[(cV, 1)])
                            f0_step(2)

                    f0_step(200)
                    mk_qevac = [None, None]
                    for u in range(4 if STAGE >= 3 else 0):
                        par5 = u % 2
                        xst = xst2[par5]
                        xTc = xTc2[par5]
                        w = []
                        if mk_trdone[par5] is not None:
                            w.append(mk_trdone[par5])
                        mk_x = P.emit(g, lambda e, u=u, xst=xst: e.dma_start(
                            out=xst.rearrange("p (n f) -> p n f", n=4),
                            in_=dxo[u * 512:(u + 1) * 512, :]
                            .rearrange("(n p) f -> p n f", p=128)),
                            waits=w, incs=[(qX2[par5], 16)], dma=True)
                        for c in range(8):
                            srcs = [xst[:, i4 * D + c * 128:i4 * D + c * 128 + 128]
                                    for i4 in range(4)]
                            ew = [(qX2[par5], mk_x[1])] if c == 0 else []
                            tr_batch(srcs,
                                     (lambda c, xTc=xTc: lambda e, psl: _cp(e,
                                         xTc[:, c * 512:(c + 1) * 512],
                                         psl[:, 0:512]))(c),
                                     ew, c % 2)
                        mk_trdone[par5] = (cT, cT.v)
                        for gi in range(4):
                            mkq2 = None
                            for half in range(2):
                                cc = 2 * gi + half
                                pq = ps_a if cc % 2 == 0 else ps_b
                                for c in range(8):
                                    w = []
                                    if c == 0:
                                        if gi == 0 and half == 0:
                                            w = [(cV, cV.v), (cA, cA.v)]
                                        if mk_qevac[cc % 2] is not None:
                                            w.extend(mk_qevac[cc % 2])
                                    P.emit(t, lambda e, c=c, cc=cc, xTc=xTc, pq=pq:
                                           e.matmul(
                                               pq[:],
                                               wq_sb[:, c * D + cc * 128:
                                                     c * D + cc * 128 + 128],
                                               xTc[:, c * 512:(c + 1) * 512],
                                               start=(c == 0), stop=(c == 7)), waits=w,
                                           incs=([(cT, 1)] if c == 7 else []))
                                mkq = (cT, cT.v)
                                j0 = 2 * half
                                P.emit(v, lambda e, gi=gi, u=u, j0=j0, pq=pq:
                                       e.tensor_copy(
                                           qn[32 * j0:32 * j0 + 32,
                                              gi * OWN + u * 512:
                                              gi * OWN + u * 512 + 512],
                                           pq[0:32, :]), waits=[mkq], incs=[(cV, 1)])
                                P.emit(v, lambda e, gi=gi, u=u, j0=j0, pq=pq:
                                       e.tensor_copy(
                                           qn[32 * (j0 + 1):32 * (j0 + 1) + 32,
                                              gi * OWN + u * 512:
                                              gi * OWN + u * 512 + 512],
                                           pq[64:96, :]), incs=[(cV, 1)])
                                P.emit(v, lambda e, pq=pq: e.tensor_copy(
                                    qfull[:], pq[:]), waits=[mkq])
                                mkq2 = P.emit(v, lambda e, half=half: e.tensor_mul(
                                    q2buf[:, half * 512:half * 512 + 512],
                                    qfull[:], qfull[:]), incs=[(cV, 1)])
                                mk_qevac[cc % 2] = [(cV, cV.v), (cA, cA.v)]
                            P.emit(t, lambda e: e.matmul(
                                ps_ss[:], qsel01[:], q2buf[:, 0:512],
                                start=True, stop=False), waits=[(cV, mkq2[1])])
                            mkss = P.emit(t, lambda e: e.matmul(
                                ps_ss[:], qsel23[:], q2buf[:, 512:1024],
                                start=False, stop=True), incs=[(cT, 1)])
                            P.emit(v, lambda e: e.tensor_scalar(
                                sm4[:, 0:512], ps_ss[:], 1.0 / 64.0, 1e-6,
                                ALU.mult, ALU.add), waits=[(cT, mkss[1])],
                                incs=[(cV, 1)])
                            P.emit(s, lambda e: e.activation(
                                sm4[:, 512:1024], sm4[:, 0:512], AF.Ln),
                                waits=[(cV, cV.v)])
                            mke = P.emit(s, lambda e: e.activation(
                                sm4[:, 1024:1536], sm4[:, 512:1024], AF.Exp,
                                scale=-0.5), incs=[(cA, 1)])
                            P.emit(v, lambda e: e.tensor_mul(
                                sm4[:, 512:1024], sm4[:, 1024:1536],
                                sm4[:, 1024:1536]), waits=[(cA, mke[1])])
                            P.emit(v, lambda e: e.tensor_mul(
                                sm4[:, 512:1024], sm4[:, 512:1024], sm4[:, 0:512]))
                            P.emit(v, lambda e: e.tensor_scalar(
                                sm4[:, 512:1024], sm4[:, 512:1024], -0.5, 1.5,
                                ALU.mult, ALU.add))
                            P.emit(v, lambda e: e.tensor_mul(
                                rkc[:], sm4[:, 1024:1536], sm4[:, 512:1024]))
                            P.emit(v, lambda e: e.tensor_copy(rkh[:], rkc[:]))
                            mk4 = P.emit(v, lambda e: e.tensor_sub(
                                rkl[:], rkc[:], rkh[:]), incs=[(cV, 1)])
                            co = gi * OWN + u * 512
                            P.emit(t, lambda e: e.matmul(
                                ps_bc[:], ind4[:], rkh[:], start=True, stop=False),
                                waits=[(cV, mk4[1])])
                            mk5 = P.emit(t, lambda e: e.matmul(
                                ps_bc[:], ind4[:], rkl[:], start=False, stop=True),
                                incs=[(cT, 1)])
                            P.emit(v, lambda e, co=co: e.scalar_tensor_tensor(
                                qn[:, co:co + 512], qn[:, co:co + 512], qg8[:, 0:1],
                                ps_bc[:], ALU.mult, ALU.mult),
                                waits=[(cT, mk5[1])], incs=[(cV, 1)])
                            mk7 = P.emit(v, lambda e, co=co: e.tensor_mul(
                                qfull[:], qn[:, co:co + 512], qn[:, co:co + 512]),
                                incs=[(cV, 1)])
                            mk8 = P.emit(t, lambda e: e.matmul(
                                ps_ss[:], ind4T[:], qfull[:], start=True, stop=True),
                                waits=[(cV, mk7[1])], incs=[(cT, 1)])
                            P.emit(v, lambda e: e.tensor_reduce(
                                sm4[:, 0:1], ps_ss[:], AX, ALU.max),
                                waits=[(cT, mk8[1])])
                            P.emit(v, lambda e, gi=gi: e.tensor_max(
                                qmax2[:, gi:gi + 1], qmax2[:, gi:gi + 1], sm4[:, 0:1]),
                                incs=[(cV, 1)])
                    if STAGE < 3:
                        P.emit(y, None, waits=[(qX2[0], qX2[0].v), (qW, qW.v)])
                    else:
                        mkp = P.emit(v, lambda e: e.tensor_mul(sm4[:, 0:4], qmax2[:], kmax2[:]),
                                     incs=[(cV, 1)])
                        mkg = P.emit(y, lambda e: e.dma_start(
                            out=sm[0:1, 0:16].rearrange("p (a b) -> p a b", a=4),
                            in_=sm4[:, 0:4].rearrange("p (n o) -> p n o", o=1),
                            allow_slow_non_contiguous=True),
                            waits=[(cV, mkp[1])], incs=[(qM, 16)])
                        P.emit(v, lambda e: e.tensor_reduce(
                            sm4[0:1, 0:1], sm[0:1, 0:16], AX, ALU.max),
                            waits=[(qM, mkg[1])], incs=[(cV, 1)])
                        P.emit(s, lambda e: e.activation(sm[0:1, 1:2], sm4[0:1, 0:1], AF.Ln),
                               waits=[(cV, cV.v)])
                        mks = P.emit(s, lambda e: e.activation(
                            sm[0:1, 2:3], sm[0:1, 1:2], AF.Exp, scale=0.5),
                            incs=[(cA, 1)])
                        mkneg = P.emit(v, lambda e: e.tensor_scalar_mul(
                            sm[0:1, 3:4], sm[0:1, 2:3], -1.01),
                            waits=[(cA, mks[1])], incs=[(cV, 1)])
                        mkb = P.emit(t, lambda e: e.matmul(
                            ps_sw[:], onesf[0:1, :], sm[0:1, 3:4],
                            start=True, stop=True), waits=[(cV, mkneg[1])], incs=[(cT, 1)])
                        P.emit(v, lambda e: e.tensor_copy(negMb[:], ps_sw[:]),
                               waits=[(cT, mkb[1])], incs=[(cV, 1)])

                    with nc.Block() as blk:
                        P.replay(blk)

            s2 = ExitStack()
            with s2:
                b2 = lambda name, shape, dt: s2.enter_context(nc.sbuf_tensor(name, shape, dt))
                woef_sb = b2("woef_sb", [128, 4 * D], BF16)
                PT = b2("PT", [128, 2048], BF16)
                PTb = b2("PTb", [128, 2048], BF16)
                outcT = b2("outcT", [128, 2048], BF16)
                den4h2 = b2("den4h2", [128, 512], BF16)
                den4l2 = b2("den4l2", [128, 512], BF16)
                den_sb = b2("den_sb", [128, 512], F32)
                bc_sb = b2("bc_sb", [128, 512], F32)
                xo_t = b2("xo_t", [128, 1024], F32)
                r_t = b2("r_t", [128, 1024], F32)
                x1_t = b2("x1_t", [128, 1024], F32)

                ps2 = ExitStack()
                with ps2:
                    pb = lambda name, shape, dt: ps2.enter_context(nc.psum_tensor(name, shape, dt))
                    ps_sc = pb("ps_sc", [128, 2048], F32)
                    ps_den = pb("ps_den", [128, 512], F32)
                    ps_pv = pb("ps_pv", [128, 512], F32)
                    ps_bc2 = pb("ps_bc2", [128, 512], F32)
                    ps_wo = pb("ps_wo", [128, 512], F32)

                    mk_wo = P.emit(g, lambda e: e.dma_start(
                        out=woef_sb.rearrange("p (c n) -> p c n", c=4),
                        in_=dwo.rearrange("(c p) n -> p c n", p=128)), incs=[(qWo, 16)], dma=True)

                    PTr = [PT.rearrange("p (j n) -> p j n", j=4),
                           PTb.rearrange("p (j n) -> p j n", j=4)]
                    PTt = [PT, PTb]
                    PSr = ps_sc.rearrange("p (j n) -> p j n", j=4)

                    mk_x1st = None
                    mk_exp_last = None
                    bnd = {"fden": None, "fpv": None}
                    chain_pend = None
                    woln_pend = []
                    for tc in range(4 if STAGE >= 4 else 0):
                        mk_exp = None
                        for gi in range(4):
                            smax = 8 * tc + 8
                            pend = None
                            mk_flush = [None, None]
                            mk_expA = None
                            for sc in range(smax):
                                j0 = sc // 2
                                stc = 128 * (j0 - 4 * tc) if sc >= 8 * tc else 0
                                pp = sc % 2
                                for j in range(4):
                                    w = []
                                    if j == 0:
                                        if mk_expA is not None:
                                            w.append(mk_expA)
                                        elif mk_exp_last is not None:
                                            w.append(mk_exp_last)
                                        else:
                                            w.append((cV, cV.v))
                                    if j == 2 and mk_exp is not None:
                                        w.append(mk_exp)
                                    P.emit(t, lambda e, j=j, gi=gi, tc=tc, sc=sc, stc=stc:
                                           e.matmul(
                                               ps_sc[:, j * 512 + stc:j * 512 + 512],
                                               khat[32 * j:32 * j + 32,
                                                    gi * T + sc * 128:
                                                    gi * T + sc * 128 + 128],
                                               qn[32 * j:32 * j + 32,
                                                  gi * OWN + tc * 512 + stc:
                                                  gi * OWN + (tc + 1) * 512],
                                               start=True, stop=True,
                                               tile_position=(32 * j, 0)),
                                           waits=w,
                                           incs=([(cT, 1)] if j in (1, 3) else []))
                                mk_scA = (cT, cT.v - 1)
                                mk_scB = (cT, cT.v)
                                if sc == 1 and chain_pend is not None:
                                    chain_pend()
                                    chain_pend = None
                                if pend is not None:
                                    pend()
                                    pend = None
                                    mk_flush[1 - pp] = (cT, cT.v)
                                w = [mk_scA]
                                if mk_flush[pp] is not None:
                                    w.append(mk_flush[pp])
                                mk_expA = P.emit(s, lambda e, stc=stc, pp=pp: e.activation(
                                    PTr[pp][:, 0:2, stc:512], PSr[:, 0:2, stc:512],
                                    AF.Exp, bias=negMb[:, 0:1]),
                                    waits=w, incs=[(cA, 1)])
                                mk_exp = P.emit(s, lambda e, stc=stc, pp=pp: e.activation(
                                    PTr[pp][:, 2:4, stc:512], PSr[:, 2:4, stc:512],
                                    AF.Exp, bias=negMb[:, 0:1]),
                                    waits=[mk_scB], incs=[(cA, 1)])
                                dep = mk_exp
                                if sc >= 8 * tc:
                                    mo = 128 * (sc % 2)
                                    dep = P.emit(v, lambda e, stc=stc, mo=mo, pp=pp:
                                                 e.tensor_tensor(
                                                     PTr[pp][:, :, stc:stc + 128],
                                                     PTr[pp][:, :, stc:stc + 128],
                                                     maskD[:, mo:mo + 128]
                                                     .rearrange("p (r n) -> p r n", r=1)
                                                     .broadcast_to((128, 4, 128)),
                                                     ALU.mult),
                                                 waits=[mk_exp], incs=[(cV, 1)])

                                def mk_pend(gi=gi, sc=sc, stc=stc, smax=smax, pp=pp,
                                            dep=dep):
                                    def emit_dp():
                                        fden = bnd["fden"]
                                        fpv = bnd["fpv"]
                                        for j in range(4):
                                            w = [dep] if j == 0 else []
                                            if j == 0 and sc == 0 and fden is not None:
                                                w.append(fden)
                                            P.emit(t, lambda e, j=j, sc=sc, stc=stc,
                                                   smax=smax, pp=pp: e.matmul(
                                                       ps_den[32 * j:32 * j + 32,
                                                              stc:512],
                                                       dencol[:],
                                                       PTt[pp][:, j * 512 + stc:
                                                               j * 512 + 512],
                                                       start=(sc == 0),
                                                       stop=(sc == smax - 1),
                                                       skip_group_check=True,
                                                       tile_position=(0, 32 * j)),
                                                   waits=w)
                                        for j in range(4):
                                            w = []
                                            if j == 0 and sc == 0 and fpv is not None:
                                                w.append(fpv)
                                            P.emit(t, lambda e, j=j, gi=gi, sc=sc,
                                                   stc=stc, smax=smax, pp=pp: e.matmul(
                                                       ps_pv[32 * j:32 * j + 32,
                                                             stc:512],
                                                       vtok[:, sc * 512
                                                            + (4 * gi + j) * 32:
                                                            sc * 512
                                                            + (4 * gi + j) * 32 + 32],
                                                       PTt[pp][:, j * 512 + stc:
                                                               j * 512 + 512],
                                                       start=(sc == 0),
                                                       stop=(sc == smax - 1),
                                                       skip_group_check=True,
                                                       tile_position=(0, 32 * j)),
                                                   waits=w,
                                                   incs=([(cT, 1)] if j == 3 else []))
                                    return emit_dp
                                pend = mk_pend()
                                if gi == 0 and 2 <= sc <= 5 and woln_pend:
                                    woln_pend.pop(0)()
                            pend()
                            pend = None
                            mk_exp_last = mk_exp
                            mk_exp = None
                            mk_mmd = (cT, cT.v)

                            def make_chain(gi=gi, mk_mmd=mk_mmd):
                                def emit_chain():
                                    mk_denevac = P.emit(
                                        v, lambda e: e.tensor_scalar_max(
                                            den_sb[:], ps_den[:], 1e-30),
                                        waits=[mk_mmd], incs=[(cV, 1)])
                                    P.emit(v, lambda e: e.reciprocal(
                                        den_sb[:], den_sb[:]))
                                    P.emit(v, lambda e: e.tensor_copy(
                                        den4h2[:], den_sb[:]))
                                    mk_hl = P.emit(v, lambda e: e.tensor_sub(
                                        den4l2[:], den_sb[:], den4h2[:]),
                                        incs=[(cV, 1)])
                                    P.emit(t, lambda e: e.matmul(
                                        ps_bc2[:], bcsel[:], den4h2[:],
                                        start=True, stop=False),
                                        waits=[(cV, mk_hl[1])])
                                    mk_bc = P.emit(t, lambda e: e.matmul(
                                        ps_bc2[:], bcsel[:], den4l2[:],
                                        start=False, stop=True), incs=[(cT, 1)])
                                    mk_bce = P.emit(v, lambda e: e.tensor_copy(
                                        bc_sb[:], ps_bc2[:]),
                                        waits=[(cT, mk_bc[1])], incs=[(cV, 1)])
                                    mk_pv = P.emit(v, lambda e, gi=gi: e.tensor_mul(
                                        outcT[:, gi * 512:gi * 512 + 512],
                                        ps_pv[:], bc_sb[:]),
                                        waits=[(cV, mk_bce[1]), (cT, cT.v)],
                                        incs=[(cV, 1)])
                                    bnd["fden"] = mk_denevac
                                    bnd["fpv"] = mk_pv
                                return emit_chain
                            chain_pend = make_chain()
                        def make_woln(tc=tc):
                            def piece(i4):
                                nonlocal mk_x1st
                                mk_xo = P.emit(y, lambda e, tc=tc, i4=i4: e.dma_start(
                                    out=xo_t[:],
                                    in_=dxo[tc * 512 + i4 * 128:
                                            tc * 512 + i4 * 128 + 128, :]),
                                    waits=[(cV, cV.v)], incs=[(qXO, 16)])
                                for fh in range(2):
                                    pdst = ps_wo if fh == 0 else ps_bc2
                                    for cg in range(4):
                                        w = []
                                        if fh == 0 and cg == 0:
                                            w = [(cV, cV.v), (qWo, mk_wo[1]),
                                                 (cA, cA.v)]
                                        P.emit(t, lambda e, cg=cg, i4=i4, fh=fh, pdst=pdst:
                                               e.matmul(
                                                   pdst[:, 0:512],
                                                   outcT[:, cg * 512 + i4 * 128:
                                                         cg * 512 + i4 * 128 + 128],
                                                   woef_sb[:, cg * D + fh * 512:
                                                           cg * D + fh * 512 + 512],
                                                   start=(cg == 0), stop=(cg == 3)),
                                               waits=w,
                                               incs=([(cT, 1)] if (cg == 3 and fh == 1)
                                                     else []))
                                mk_y = (cT, cT.v)
                                P.emit(v, lambda e: e.scalar_tensor_tensor(
                                    r_t[:, 0:512], ps_wo[:, 0:512], 1.0, xo_t[:, 0:512],
                                    ALU.mult, ALU.add, accum_out=sm[:, 0:1]),
                                    waits=[mk_y, (qXO, mk_xo[1])])
                                mk_r = P.emit(v, lambda e: e.scalar_tensor_tensor(
                                    r_t[:, 512:1024], ps_bc2[:, 0:512], 1.0,
                                    xo_t[:, 512:1024], ALU.mult, ALU.add,
                                    accum_out=sm[:, 1:2]), incs=[(cV, 1)])
                                P.emit(v, lambda e: e.scalar_tensor_tensor(
                                    xo_t[:], r_t[:], 1.0, r_t[:], ALU.mult, ALU.mult,
                                    accum_out=sm[:, 2:3]))
                                P.emit(v, lambda e: e.tensor_add(
                                    sm[:, 0:1], sm[:, 0:1], sm[:, 1:2]))
                                P.emit(v, lambda e: e.tensor_scalar_mul(
                                    sm[:, 0:1], sm[:, 0:1], 1.0 / 1024.0))
                                P.emit(v, lambda e: e.tensor_scalar_mul(
                                    sm[:, 2:3], sm[:, 2:3], 1.0 / 1024.0))
                                P.emit(v, lambda e: e.tensor_mul(
                                    sm[:, 3:4], sm[:, 0:1], sm[:, 0:1]))
                                P.emit(v, lambda e: e.tensor_sub(
                                    sm[:, 2:3], sm[:, 2:3], sm[:, 3:4]))
                                mk_v = P.emit(v, lambda e: e.tensor_scalar_add(
                                    sm[:, 2:3], sm[:, 2:3], 1e-5), incs=[(cV, 1)])
                                P.emit(s, lambda e: e.activation(
                                    sm[:, 4:5], sm[:, 2:3], AF.Ln), waits=[(cV, mk_v[1])])
                                mke = P.emit(s, lambda e: e.activation(
                                    sm[:, 5:6], sm[:, 4:5], AF.Exp, scale=-0.5),
                                    incs=[(cA, 1)])
                                P.emit(v, lambda e: e.tensor_mul(
                                    sm[:, 6:7], sm[:, 5:6], sm[:, 5:6]),
                                    waits=[(cA, mke[1])])
                                P.emit(v, lambda e: e.tensor_mul(
                                    sm[:, 6:7], sm[:, 6:7], sm[:, 2:3]))
                                P.emit(v, lambda e: e.tensor_scalar(
                                    sm[:, 6:7], sm[:, 6:7], -0.5, 1.5, ALU.mult, ALU.add))
                                P.emit(v, lambda e: e.tensor_mul(
                                    sm[:, 5:6], sm[:, 5:6], sm[:, 6:7]))
                                w = [(qX1, mk_x1st[1])] if mk_x1st is not None else []
                                P.emit(v, lambda e: e.tensor_scalar(
                                    x1_t[:], r_t[:], sm[:, 0:1], None, ALU.subtract),
                                    waits=w)
                                mk_x1 = P.emit(v, lambda e: e.tensor_scalar(
                                    x1_t[:], x1_t[:], sm[:, 5:6], None, ALU.mult),
                                    incs=[(cV, 1)])
                                mk_x1st = P.emit(y, lambda e, tc=tc, i4=i4: e.dma_start(
                                    out=sx1[tc * 512 + i4 * 128:
                                            tc * 512 + i4 * 128 + 128, :],
                                    in_=x1_t[:]), waits=[(cV, mk_x1[1])], incs=[(qX1, 16)])
                            return [lambda i4=i4: piece(i4) for i4 in range(4)]
                        woln_pend = make_woln()
                        if tc == 3:
                            chain_pend()
                            chain_pend = None
                            while woln_pend:
                                woln_pend.pop(0)()

                    with nc.Block() as blk:
                        P.replay(blk)

        s3 = ExitStack()
        with s3:
            b3 = lambda name, shape, dt: s3.enter_context(nc.sbuf_tensor(name, shape, dt))
            x1f = b3("x1f", [128, 4 * D], F32)
            xq = b3("xq", [128, 4 * D], BF16)
            xqT = b3("xqT", [128, 4 * D], BF16)
            wgbuf = [b3("wgbuf0", [128, 8 * 512], BF16),
                     b3("wgbuf1", [128, 8 * 512], BF16)]
            wubuf = [b3("wubuf0", [128, 8 * 512], BF16),
                     b3("wubuf1", [128, 8 * 512], BF16)]
            wdbuf = [b3("wdbuf0", [128, 1024], BF16),
                     b3("wdbuf1", [128, 1024], BF16)]
            gate_s2 = [b3("gate_s0", [128, 512], F32),
                       b3("gate_s1", [128, 512], F32)]
            hbuf = b3("hbuf", [128, 4 * INNER], F32)
            hq = b3("hq", [128, INNER], BF16)
            hqT = b3("hqT", [128, 4 * INNER], BF16)
            r2f = b3("r2f", [128, 4 * D], F32)
            y_t = b3("y_t", [128, D], F32)
            xsc = b3("xsc", [128, 32], F32)
            xsc2 = b3("xsc2", [128, 8], F32)

            ps3 = ExitStack()
            with ps3:
                pb = lambda name, shape, dt: ps3.enter_context(nc.psum_tensor(name, shape, dt))
                ps_f = pb("ps_f", [128, 3072], F32)
                ps_tf = [pb("ps_tf0", [128, 1024], BF16),
                         pb("ps_tf1", [128, 1024], BF16)]

                tfe = [None, None]

                def ftr_batch(srcs, dst_fn, extra_wait, bank, eng=None):
                    for k in range(4):
                        w = []
                        if k == 0:
                            w += extra_wait
                            if tfe[bank] is not None:
                                w.append(tfe[bank])
                        P.emit(t, lambda e, k=k, bank=bank, src=srcs[k]: e.transpose(
                            ps_tf[bank][:, k * 128:(k + 1) * 128], src, ident[:]),
                            waits=w, incs=([(cT, 1)] if k == 3 else []))
                    e2, ctr = (v, cV) if bank == 0 else (s, cA)
                    if eng == "s":
                        e2, ctr = s, cA
                    mk = P.emit(e2, lambda e, bank=bank: dst_fn(e, ps_tf[bank]),
                                waits=[(cT, cT.v)], incs=[(ctr, 1)])
                    tfe[bank] = mk
                    return mk

                mk_wgu = [None, None]
                mk_wdu = [None, None]
                mk_yst = None
                mk_sil2 = [None, None]
                mk_h2 = [None, None]
                mk_dev_last = [None]

                def emit_xprep(tc):
                    w_ld = []
                    if tc == 0:
                        w_ld += [(qX1, qX1.v), (qP[0], qP[0].v), (qP[1], qP[1].v)]
                    if mk_dev_last[0] is not None:
                        w_ld.append(mk_dev_last[0])
                    mk_ld = P.emit(y, lambda e, tc=tc: e.dma_start(
                        out=x1f.rearrange("p (n f) -> p n f", n=4),
                        in_=sx1[tc * 512:(tc + 1) * 512, :]
                        .rearrange("(n p) f -> p n f", p=128)),
                        waits=w_ld, incs=[(qXL, 16)])
                    for i4 in range(4):
                        ti = tc * 4 + i4
                        c8 = 8 * i4
                        P.emit(v, lambda e, i4=i4, c8=c8: e.tensor_reduce(
                            xsc[:, c8:c8 + 1], x1f[:, i4 * D:(i4 + 1) * D],
                            AX, ALU.max, apply_absolute_value=True),
                            waits=[(qXL, mk_ld[1]), (cT, cT.v)])
                        P.emit(v, lambda e, c8=c8: e.tensor_scalar_max(
                            xsc[:, c8:c8 + 1], xsc[:, c8:c8 + 1], 1e-5))
                        P.emit(v, lambda e, c8=c8: e.reciprocal(
                            xsc[:, c8 + 1:c8 + 2], xsc[:, c8:c8 + 1]))
                        P.emit(v, lambda e, c8=c8: e.tensor_scalar_mul(
                            xsc[:, c8 + 1:c8 + 2], xsc[:, c8 + 1:c8 + 2], 127.0))
                        P.emit(v, lambda e, c8=c8: e.tensor_scalar_mul(
                            xsc[:, c8 + 2:c8 + 3], xsc[:, c8:c8 + 1], 1.0 / 127.0))
                        P.emit(v, lambda e, c8=c8: e.tensor_mul(
                            xsc[:, c8 + 3:c8 + 4], swb[:, 0:1], xsc[:, c8 + 2:c8 + 3]))
                        mk_scl = P.emit(v, lambda e, c8=c8: e.tensor_mul(
                            xsc[:, c8 + 4:c8 + 5], swb[:, 1:2], xsc[:, c8 + 2:c8 + 3]),
                            incs=[(cV, 1)])
                        P.emit(v, lambda e, i4=i4, c8=c8: e.tensor_scalar(
                            hbuf[:, 0:D], x1f[:, i4 * D:(i4 + 1) * D],
                            xsc[:, c8 + 1:c8 + 2], MAGIC, ALU.mult, ALU.add))
                        mk_xq = P.emit(v, lambda e, i4=i4: e.tensor_scalar(
                            xq[:, i4 * D:(i4 + 1) * D], hbuf[:, 0:D],
                            MAGIC, 127.0, ALU.subtract, ALU.min),
                            incs=[(cV, 1)])
                        for b in range(2):
                            ew = [(cV, mk_xq[1])] if b == 0 else []
                            if tc == 0 and i4 == 0 and b == 0:
                                ew.append((cA, cA.v))
                            srcs = [xq[:, i4 * D + (4 * b + k) * 128:
                                       i4 * D + (4 * b + k) * 128 + 128]
                                    for k in range(4)]
                            ftr_batch(srcs,
                                      (lambda i4, b: lambda e, psl: _cp(e,
                                          xqT[:, i4 * D + 4 * b * 128:
                                              i4 * D + 4 * b * 128 + 512],
                                          psl[:, 0:512]))(i4, b),
                                      ew, b)
                    return mk_scl

                mk_tr4 = [None, None, None, None]
                if STAGE >= 5:
                    mk_scl = emit_xprep(0)
                for tc in range(4 if STAGE >= 5 else 0):
                    for io in range(8):
                        bi = io % 2
                        w = []
                        if mk_wgu[bi] is not None:
                            w.append(mk_wgu[bi])
                        mk_g = P.emit(y, lambda e, io=io, bi=bi: e.dma_start(
                            out=wgbuf[bi].rearrange("p (c n) -> p c n", c=8)
                            [:, 0:4, :],
                            in_=sg.rearrange("(c p) n -> p c n", p=128)
                            [:, 0:4, io * 512:(io + 1) * 512]),
                            waits=w, incs=[(qGU[bi], 16)])
                        mk_g2 = P.emit(y, lambda e, io=io, bi=bi: e.dma_start(
                            out=wgbuf[bi].rearrange("p (c n) -> p c n", c=8)
                            [:, 4:8, :],
                            in_=sg.rearrange("(c p) n -> p c n", p=128)
                            [:, 4:8, io * 512:(io + 1) * 512]),
                            incs=[(qGUb[bi], 16)])
                        mk_u = P.emit(y, lambda e, io=io, bi=bi: e.dma_start(
                            out=wubuf[bi].rearrange("p (c n) -> p c n", c=8)
                            [:, 0:4, :],
                            in_=su.rearrange("(c p) n -> p c n", p=128)
                            [:, 0:4, io * 512:(io + 1) * 512]),
                            incs=[(qGUc[bi], 16)])
                        mk_u2 = P.emit(y, lambda e, io=io, bi=bi: e.dma_start(
                            out=wubuf[bi].rearrange("p (c n) -> p c n", c=8)
                            [:, 4:8, :],
                            in_=su.rearrange("(c p) n -> p c n", p=128)
                            [:, 4:8, io * 512:(io + 1) * 512]),
                            incs=[(qGUd[bi], 16)])
                        for i4 in range(4):
                            c8 = 8 * i4
                            par = (io * 4 + i4) % 2
                            gb = par * 2048
                            ub = 512 + par * 2048
                            for c in range(8):
                                w = []
                                if c == 0:
                                    w = [(qGU[bi], mk_g[1])]
                                    if io == 0 and i4 == 0:
                                        w.append((cA, cA.v))
                                        if mk_dev_last[0] is not None:
                                            w.append(mk_dev_last[0])
                                    if mk_sil2[par] is not None:
                                        w.append(mk_sil2[par])
                                    if mk_h2[par] is not None:
                                        w.append(mk_h2[par])
                                if c == 4 and i4 == 0:
                                    w.append((qGUb[bi], mk_g2[1]))
                                P.emit(t, lambda e, c=c, i4=i4, bi=bi, gb=gb: e.matmul(
                                    ps_f[:, gb:gb + 512],
                                    xqT[:, i4 * D + c * 128:i4 * D + c * 128 + 128],
                                    wgbuf[bi][:, c * 512:(c + 1) * 512],
                                    start=(c == 0), stop=(c == 7)), waits=w)
                            for c in range(8):
                                w = []
                                if c == 0 and i4 == 0:
                                    w.append((qGUc[bi], mk_u[1]))
                                if c == 4 and i4 == 0:
                                    w.append((qGUd[bi], mk_u2[1]))
                                P.emit(t, lambda e, c=c, i4=i4, bi=bi, ub=ub: e.matmul(
                                    ps_f[:, ub:ub + 512],
                                    xqT[:, i4 * D + c * 128:i4 * D + c * 128 + 128],
                                    wubuf[bi][:, c * 512:(c + 1) * 512],
                                    start=(c == 0), stop=(c == 7)), waits=w,
                                    incs=([(cT, 1)] if c == 7 else []))
                            mk_gu = (cT, cT.v)
                            mk_sil2[par] = P.emit(s, lambda e, c8=c8, gb=gb, par=par:
                                e.activation(
                                    gate_s2[par][:], ps_f[:, gb:gb + 512], AF.Silu,
                                    scale=xsc[:, c8 + 3:c8 + 4]),
                                waits=[mk_gu, (cV, mk_scl[1])], incs=[(cA, 1)])
                            mk_h2[par] = P.emit(v, lambda e, i4=i4, io=io, c8=c8,
                                                ub=ub, par=par:
                                          e.scalar_tensor_tensor(
                                              hbuf[:, i4 * INNER + io * 512:
                                                   i4 * INNER + io * 512 + 512],
                                              ps_f[:, ub:ub + 512],
                                              xsc[:, c8 + 4:c8 + 5], gate_s2[par][:],
                                              ALU.mult, ALU.mult),
                                          waits=[(cA, mk_sil2[par][1])], incs=[(cV, 1)])
                            if io == 0:
                                P.emit(v, lambda e, i4=i4, io=io, c8=c8:
                                       e.tensor_reduce(
                                           xsc[:, c8 + 5:c8 + 6],
                                           hbuf[:, i4 * INNER + io * 512:
                                                i4 * INNER + io * 512 + 512],
                                           AX, ALU.max, apply_absolute_value=True))
                            else:
                                P.emit(v, lambda e, i4=i4, io=io: e.tensor_reduce(
                                    sm[:, 9:10],
                                    hbuf[:, i4 * INNER + io * 512:
                                         i4 * INNER + io * 512 + 512],
                                    AX, ALU.max, apply_absolute_value=True))
                                P.emit(v, lambda e, c8=c8: e.tensor_max(
                                    xsc[:, c8 + 5:c8 + 6], xsc[:, c8 + 5:c8 + 6],
                                    sm[:, 9:10]))
                        mk_wgu[bi] = (cT, cT.v)
                    for i4 in range(4):
                        c8 = 8 * i4
                        P.emit(v, lambda e, c8=c8: e.tensor_scalar_max(
                            xsc[:, c8 + 5:c8 + 6], xsc[:, c8 + 5:c8 + 6], 1e-5))
                        P.emit(v, lambda e, c8=c8: e.reciprocal(
                            xsc[:, c8 + 6:c8 + 7], xsc[:, c8 + 5:c8 + 6]))
                        P.emit(v, lambda e, c8=c8: e.tensor_scalar_mul(
                            xsc[:, c8 + 6:c8 + 7], xsc[:, c8 + 6:c8 + 7], 127.0))
                        P.emit(v, lambda e, c8=c8: e.tensor_scalar_mul(
                            xsc[:, c8 + 7:c8 + 8], xsc[:, c8 + 5:c8 + 6],
                            1.0 / 127.0), incs=[(cV, 1)])
                    mk_dev = None
                    for fh in range(2):
                        for ic in range(16):
                            bi = ic % 2
                            if fh == 0 and ic % 2 == 0:
                                b8 = ic // 2
                                for i4 in range(4):
                                    c8 = 8 * i4
                                    hs = i4 * INNER + b8 * 512
                                    P.emit(v, lambda e, hs=hs, c8=c8: e.tensor_scalar(
                                        hbuf[:, hs:hs + 512], hbuf[:, hs:hs + 512],
                                        xsc[:, c8 + 6:c8 + 7], MAGIC,
                                        ALU.mult, ALU.add))
                                    mk_hq = P.emit(v, lambda e, i4=i4, hs=hs:
                                                   e.tensor_scalar(
                                                       hq[:, i4 * 512:i4 * 512 + 512],
                                                       hbuf[:, hs:hs + 512],
                                                       MAGIC, 127.0,
                                                       ALU.subtract, ALU.min),
                                                   waits=([mk_tr4[i4]]
                                                          if mk_tr4[i4] else []),
                                                   incs=[(cV, 1)])
                                    srcs = [hq[:, i4 * 512 + k * 128:
                                               i4 * 512 + k * 128 + 128]
                                            for k in range(4)]
                                    ftr_batch(srcs,
                                              (lambda i4, b8: lambda e, psl: _cp(e,
                                                  hqT[:, i4 * INNER + b8 * 512:
                                                      i4 * INNER + b8 * 512 + 512],
                                                  psl[:, 0:512]))(i4, b8),
                                              [(cV, mk_hq[1])], i4 % 2,
                                              eng="s")
                                    mk_tr4[i4] = (cT, cT.v)
                            w = []
                            if mk_wdu[bi] is not None:
                                w.append(mk_wdu[bi])
                            mk_wd = P.emit(y if bi == 0 else "scalar",
                                           lambda e, ic=ic, fh=fh, bi=bi: e.dma_start(
                                out=wdbuf[bi][:, 0:512],
                                in_=sd[ic * 256:ic * 256 + 128,
                                       fh * 512:(fh + 1) * 512]),
                                waits=w, incs=[(qWD[bi], 16)], dma=True)
                            mk_wd2 = P.emit(y if bi == 0 else "scalar",
                                            lambda e, ic=ic, fh=fh, bi=bi:
                                            e.dma_start(
                                out=wdbuf[bi][:, 512:1024],
                                in_=sd[ic * 256 + 128:(ic + 1) * 256,
                                       fh * 512:(fh + 1) * 512]),
                                incs=[(qWD2[bi], 16)], dma=True)
                            for sub in range(2):
                                io = ic * 2 + sub
                                for i4 in range(4):
                                    w = ([(qWD[bi], mk_wd[1])]
                                         if (i4 == 0 and sub == 0) else [])
                                    if i4 == 0 and sub == 1:
                                        w.append((qWD2[bi], mk_wd2[1]))
                                    if (i4 == 0 and sub == 0 and fh == 0
                                            and ic % 2 == 0):
                                        w.append((cV, cV.v))
                                        w.append((cA, cA.v))
                                    if io == 0 and i4 == 0:
                                        if fh == 1:
                                            w.append((cV, cV.v))
                                            w.append((cA, cA.v))
                                        if mk_dev is not None:
                                            w.append(mk_dev)
                                    P.emit(t, lambda e, i4=i4, io=io, bi=bi, sub=sub:
                                           e.matmul(
                                               ps_f[:, ((2 + i4) % 4) * 512:
                                                    ((2 + i4) % 4) * 512 + 512],
                                               hqT[:, i4 * INNER + io * 128:
                                                   i4 * INNER + io * 128 + 128],
                                               wdbuf[bi][:, sub * 512:sub * 512 + 512],
                                               start=(io == 0), stop=(io == 31)),
                                           waits=w,
                                           incs=([(cT, 1)] if (i4 == 3 and sub == 1)
                                                 else []))
                            mk_wdu[bi] = (cT, cT.v)
                        mk_dn = (cT, cT.v)
                        for i4 in range(4):
                            c8 = 8 * i4
                            P.emit(v, lambda e, c8=c8: e.tensor_mul(
                                sm[:, 8:9], swb[:, 2:3], xsc[:, c8 + 7:c8 + 8]),
                                waits=[mk_dn] if i4 == 0 else [])
                            mk_dev = P.emit(v, lambda e, i4=i4, fh=fh:
                                            e.scalar_tensor_tensor(
                                                r2f[:, i4 * D + fh * 512:
                                                    i4 * D + fh * 512 + 512],
                                                ps_f[:, ((2 + i4) % 4) * 512:
                                                     ((2 + i4) % 4) * 512 + 512],
                                                sm[:, 8:9],
                                                x1f[:, i4 * D + fh * 512:
                                                    i4 * D + fh * 512 + 512],
                                                ALU.mult, ALU.add,
                                                accum_out=xsc2[:, 2 * i4 + fh:
                                                               2 * i4 + fh + 1]),
                                            incs=[(cV, 1)])
                    mk_dev_last[0] = mk_dev
                    if tc < 3:
                        mk_scl = emit_xprep(tc + 1)
                    for i4 in range(4):
                        P.emit(v, lambda e, i4=i4: e.scalar_tensor_tensor(
                            hbuf[:, 0:D], r2f[:, i4 * D:(i4 + 1) * D], 1.0,
                            r2f[:, i4 * D:(i4 + 1) * D], ALU.mult, ALU.mult,
                            accum_out=sm[:, 2:3]))
                        P.emit(v, lambda e, i4=i4: e.tensor_add(
                            sm[:, 0:1], xsc2[:, 2 * i4:2 * i4 + 1],
                            xsc2[:, 2 * i4 + 1:2 * i4 + 2]))
                        P.emit(v, lambda e: e.tensor_scalar_mul(
                            sm[:, 0:1], sm[:, 0:1], 1.0 / 1024.0))
                        P.emit(v, lambda e: e.tensor_scalar_mul(
                            sm[:, 2:3], sm[:, 2:3], 1.0 / 1024.0))
                        P.emit(v, lambda e: e.tensor_mul(
                            sm[:, 3:4], sm[:, 0:1], sm[:, 0:1]))
                        P.emit(v, lambda e: e.tensor_sub(
                            sm[:, 2:3], sm[:, 2:3], sm[:, 3:4]))
                        mk_v = P.emit(v, lambda e: e.tensor_scalar_add(
                            sm[:, 2:3], sm[:, 2:3], 1e-5), incs=[(cV, 1)])
                        P.emit(s, lambda e: e.activation(
                            sm[:, 4:5], sm[:, 2:3], AF.Ln), waits=[(cV, mk_v[1])])
                        mke = P.emit(s, lambda e: e.activation(
                            sm[:, 5:6], sm[:, 4:5], AF.Exp, scale=-0.5), incs=[(cA, 1)])
                        P.emit(v, lambda e: e.tensor_mul(
                            sm[:, 6:7], sm[:, 5:6], sm[:, 5:6]), waits=[(cA, mke[1])])
                        P.emit(v, lambda e: e.tensor_mul(
                            sm[:, 6:7], sm[:, 6:7], sm[:, 2:3]))
                        P.emit(v, lambda e: e.tensor_scalar(
                            sm[:, 6:7], sm[:, 6:7], -0.5, 1.5, ALU.mult, ALU.add))
                        P.emit(v, lambda e: e.tensor_mul(
                            sm[:, 5:6], sm[:, 5:6], sm[:, 6:7]))
                        w = [(qY, mk_yst[1])] if mk_yst is not None else []
                        P.emit(v, lambda e, i4=i4: e.tensor_scalar(
                            y_t[:], r2f[:, i4 * D:(i4 + 1) * D], sm[:, 0:1], None,
                            ALU.subtract), waits=w)
                        mk_yv = P.emit(v, lambda e: e.tensor_scalar(
                            y_t[:], y_t[:], sm[:, 5:6], None, ALU.mult),
                            incs=[(cV, 1)])
                        mk_yst = P.emit(y, lambda e, tc=tc, i4=i4: e.dma_start(
                            out=dy[tc * 512 + i4 * 128:tc * 512 + i4 * 128 + 128, :],
                            in_=y_t[:]), waits=[(cV, mk_yv[1])], incs=[(qY, 16)])

                P.emit(y, None, waits=[(qY, qY.v)])

                with nc.Block() as blk:
                    P.replay(blk)

    return nc


_NC_CACHE = None


def _get_nc():
    global _NC_CACHE
    if _NC_CACHE is None:
        _NC_CACHE = build_nc()
    return _NC_CACHE


def kernel(L0_attn_x, L0_attn_Wq, L0_attn_Wkv, L0_attn_Wk, L0_attn_Wv, L0_attn_Wo,
           L0_attn_q_g, L0_attn_k_g, L0_ffn_w_gate, L0_ffn_w_up, L0_ffn_w_down):
    x = np.ascontiguousarray(np.asarray(L0_attn_x, dtype=np.float32))
    wq = np.ascontiguousarray(np.asarray(L0_attn_Wq, dtype=np.float32))
    wkv = np.ascontiguousarray(np.asarray(L0_attn_Wkv, dtype=np.float32))
    wk = np.ascontiguousarray(np.asarray(L0_attn_Wk, dtype=np.float32))
    wv = np.ascontiguousarray(np.asarray(L0_attn_Wv, dtype=np.float32))
    wo = np.asarray(L0_attn_Wo, dtype=np.float32)
    woef = np.ascontiguousarray(
        wo.reshape(16, 64, D)[:, :32, :].reshape(512, D))
    qg = np.asarray(L0_attn_q_g, dtype=np.float32)
    kg = np.asarray(L0_attn_k_g, dtype=np.float32)
    qg4r = np.ascontiguousarray(np.tile(qg[:32], 4).reshape(128, 1))
    kg4 = np.ascontiguousarray(np.tile(kg, 4).reshape(128, 1))
    wgT = np.ascontiguousarray(np.asarray(L0_ffn_w_gate, dtype=np.float32).T)
    wuT = np.ascontiguousarray(np.asarray(L0_ffn_w_up, dtype=np.float32).T)
    wdT = np.ascontiguousarray(np.asarray(L0_ffn_w_down, dtype=np.float32).T)

    own_idx = {h: np.concatenate([np.arange(128 * (2 * j + h), 128 * (2 * j + h) + 128)
                                  for j in range(16)]) for h in range(2)}

    in_maps = []
    for core in range(8):
        b, h = core // 2, core % 2
        xo = np.ascontiguousarray(x[b][own_idx[h]])
        in_maps.append({
            "x": x[b], "xo": xo, "wq": wq, "wkv": wkv, "wk": wk, "wv": wv,
            "woef": woef, "wgT": wgT, "wuT": wuT, "wdT": wdT,
            "qg4r": qg4r, "kg4": kg4,
            "hflag": np.full((128, 1), float(h), dtype=np.float32),
        })

    nc = _get_nc()
    res = run_bass_kernel_spmd(nc, in_maps, core_ids=list(range(8)))

    out = np.empty((4, T, D), dtype=np.float32)
    for core in range(8):
        b, h = core // 2, core % 2
        out[b][own_idx[h]] = res.results[core]["y"]
    return out
